# version 19 (final)
# baseline (speedup 1.0000x reference)
import contextlib
import numpy as np
import concourse.bass as bass
import concourse.mybir as mybir
from concourse.bass_utils import run_bass_kernel_spmd

F32 = mybir.dt.float32
F16 = mybir.dt.float16
BF16 = mybir.dt.bfloat16
AF = mybir.ActivationFunctionType
ALU = mybir.AluOpType
AX = mybir.AxisListType.X

T = 2048
D = 2048
DFF = 8192
DIN = 2048
NST = 16
DTR = 128
KCV = 4
NTC = T // 128
NDC = D // 128
NFC = DFF // 128
NFG = DFF // 512

BUILD_PHASES = 3


class Ctx:
    def __init__(self, nc, es):
        self.nc = nc
        self.es = es
        self.sems = {}
        self.cnt = {}

    def sem(self, name):
        if name not in self.sems:
            self.sems[name] = self.es.enter_context(self.nc.semaphore(name))
            self.cnt[name] = 0
        return self.sems[name]

    def guard(self, eng, name, ins):
        ins.then_inc(self.sem(name), 1)
        self.cnt[name] += 1
        eng.wait_ge(self.sems[name], self.cnt[name])


def chunked_src(dram_ap, row0, nrows, col0, ncols, row_stride):
    nchunk = nrows // 128
    off = row0 * row_stride + col0
    return bass.AP(dram_ap.tensor, off,
                   [[row_stride, 128], [128 * row_stride, nchunk], [1, ncols]])


def rows_src(dram_ap, row0, nrows, ncols, row_stride, col0=0):
    return bass.AP(dram_ap.tensor, row0 * row_stride + col0,
                   [[row_stride, nrows], [1, ncols]])


def _emit_p0(c, io, g, scr):
    nc = c.nc
    NT1 = NDC
    NT2 = NFC // 4
    NTILES = NT1 + NT2
    nconst = 2 + 4 * NDC
    P = "f1"
    x_src = io["x_in"]
    with contextlib.ExitStack() as ps:
        buf = [ps.enter_context(nc.sbuf_tensor(f"p0_buf{i}", [128, DFF], F32)) for i in range(2)]
        dump = ps.enter_context(nc.sbuf_tensor("p0_dump", [128, DFF], F32))
        acc = ps.enter_context(nc.sbuf_tensor("p0_acc", [128, NTILES], F32))
        col = ps.enter_context(nc.sbuf_tensor("p0_col", [128, 8], F32))
        onec = ps.enter_context(nc.sbuf_tensor("p0_onec", [128, 1], F32))
        pscl = ps.enter_context(nc.psum_tensor("p0_ps", [128, 16], F32))
        xa = [ps.enter_context(nc.sbuf_tensor(f"{P}a_x{i}", [128, D], F32)) for i in range(2)]
        xln = [ps.enter_context(nc.sbuf_tensor(f"{P}a_ln{i}", [128, D], F32)) for i in range(2)]
        sqd = ps.enter_context(nc.sbuf_tensor(f"{P}a_sqd", [128, D], F32))
        q16 = ps.enter_context(nc.sbuf_tensor(f"{P}a_q16", [128, D], F16))
        iqb = [ps.enter_context(nc.sbuf_tensor(f"{P}a_iq{i}", [128, D], BF16)) for i in range(2)]
        cl = ps.enter_context(nc.sbuf_tensor(f"{P}a_col", [128, 8], F32))

        with nc.Block("p0") as block:
            @block.sync
            def _(sync):
                sync.dma_start(g["eye"][:], io["eye"]).then_inc(c.sem("p0ld"), 16)
                sync.dma_start(g["ones1"][:], io["ones1"]).then_inc(c.sem("p0ld"), 16)
                for dc in range(NDC):
                    sync.dma_start(g["acol"][:, dc * NST:(dc + 1) * NST],
                                   rows_src(io["alog"], dc * 128, 128, NST, NST)
                                   ).then_inc(c.sem("p0ld"), 16)
                    sync.dma_start(g["bdt"][:, dc:dc + 1],
                                   rows_src(io["bdt"], dc * 128, 128, 1, 1)
                                   ).then_inc(c.sem("p0ld"), 16)
                    sync.dma_start(g["convb"][:, dc:dc + 1],
                                   rows_src(io["convb"], dc * 128, 128, 1, 1)
                                   ).then_inc(c.sem("p0ld"), 16)
                    sync.dma_start(g["dvec"][:, dc:dc + 1],
                                   rows_src(io["dvec"], dc * 128, 128, 1, 1)
                                   ).then_inc(c.sem("p0ld"), 16)
                for i in range(NTILES):
                    if i >= 2:
                        sync.wait_ge(c.sem("p0f"), i)
                    if i < NT1:
                        srcs = [chunked_src(io["w1t"], i * 128, 128, h * (DFF // 2),
                                            DFF // 2, DFF) for h in range(2)]
                    else:
                        srcs = [chunked_src(io["w2t"], (i - NT1) * 512 + h * 256, 256,
                                            0, D, D) for h in range(2)]
                    for h in range(2):
                        sync.dma_start(buf[i % 2][:, h * (DFF // 2):(h + 1) * (DFF // 2)],
                                       srcs[h]).then_inc(c.sem("p0w"), 16)
                    tc = i - 2
                    if 0 <= tc < NTC:
                        if tc >= 2:
                            sync.wait_ge(c.sem(P + "av"), 3 * (tc - 2) + 3)
                        sync.dma_start(xa[tc % 2][:], rows_src(x_src, tc * 128, 128, D, D)
                                       ).then_inc(c.sem(P + "ax" + str(tc % 2)), 16)

            @block.gpsimd
            def _(gp):
                for tc in range(NTC):
                    gp.wait_ge(c.sem(P + "av"), 3 * tc + 3)
                    gp.dma_start(rows_src(scr["iq"], tc * 128, 128, D, D), iqb[tc % 2][:]
                                 ).then_inc(c.sem(P + "ast"), 16)

            @block.vector
            def _(vector):
                vector.memset(g["c1536"][:], 1536.0)
                vector.memset(g["ceps"][:], 1e-5)
                vector.memset(g["hstate"][:], 0.0)
                vector.memset(onec[:], 1.0)

            @block.scalar
            def _(scalar):
                def trio(tc):
                    scalar.wait_ge(c.sem(P + "ax" + str(tc % 2)), 16 * (tc // 2 + 1))
                    scalar.activation(sqd[:], xa[tc % 2][:], AF.Square
                                      ).then_inc(c.sem(P + "as"), 1)
                    scalar.wait_ge(c.sem(P + "av"), 3 * tc + 1)
                    scalar.activation(cl[:, 4:5], cl[:, 3:4], AF.Sqrt,
                                      bias=g["ceps"][:, 0:1]).then_inc(c.sem(P + "as"), 1)
                    scalar.wait_ge(c.sem(P + "av"), 3 * tc + 2)
                    scalar.activation(xln[tc % 2][:], xa[tc % 2][:], AF.Identity,
                                      scale=cl[:, 4:5], bias=cl[:, 5:6]
                                      ).then_inc(c.sem(P + "as"), 1)

                scalar.wait_ge(c.sem("p0ld"), 16 * nconst)
                scalar.activation(g["acol"][:], g["acol"][:], AF.Exp
                                  ).then_inc(c.sem("p0f"), 1)
                for i in range(NTILES):
                    scalar.wait_ge(c.sem("p0w"), 32 * (i + 1))
                    scalar.activation(dump[:], buf[i % 2][:], AF.Abs,
                                      accum_out=acc[:, i:i + 1]).then_inc(c.sem("p0f"), 1)
                    if 0 <= i - 2 < NTC:
                        trio(i - 2)
                scalar.activation(dump[:, 0:128], dump[:, 0:128], AF.Identity)
                scalar.activation(dump[:, 0:128], dump[:, 0:128], AF.Identity
                                  ).then_inc(c.sem("p0f"), 1)

            @block.vector
            def _(vector):
                G = "gv"
                for tc in range(NTC):
                    vector.wait_ge(c.sem(P + "ax" + str(tc % 2)), 16 * (tc // 2 + 1))
                    c.guard(vector, G, vector.reduce_sum(cl[:, 0:1], xa[tc % 2][:], AX))
                    vector.wait_ge(c.sem(P + "as"), 3 * tc + 1)
                    c.guard(vector, G, vector.reduce_sum(cl[:, 1:2], sqd[:], AX))
                    c.guard(vector, G, vector.tensor_scalar_mul(cl[:, 2:3], cl[:, 0:1], 1.0 / D))
                    c.guard(vector, G, vector.tensor_scalar_mul(cl[:, 3:4], cl[:, 1:2], 1.0 / D))
                    c.guard(vector, G, vector.tensor_mul(cl[:, 0:1], cl[:, 2:3], cl[:, 2:3]))
                    vector.tensor_sub(cl[:, 3:4], cl[:, 3:4], cl[:, 0:1]
                                      ).then_inc(c.sem(P + "av"), 1)
                    vector.wait_ge(c.sem(P + "as"), 3 * tc + 2)
                    c.guard(vector, G, vector.reciprocal(cl[:, 4:5], cl[:, 4:5]))
                    c.guard(vector, G, vector.tensor_mul(cl[:, 5:6], cl[:, 2:3], cl[:, 4:5]))
                    vector.tensor_scalar_mul(cl[:, 5:6], cl[:, 5:6], -1.0
                                             ).then_inc(c.sem(P + "av"), 1)
                    vector.wait_ge(c.sem(P + "as"), 3 * tc + 3)
                    if tc >= 2:
                        vector.wait_ge(c.sem(P + "ast"), 16 * (tc - 1))
                    c.guard(vector, G, vector.reduce_max(cl[:, 6:7], xln[tc % 2][:], AX,
                                                         apply_absolute_value=True))
                    c.guard(vector, G, vector.tensor_scalar_max(cl[:, 6:7], cl[:, 6:7], 1e-5))
                    c.guard(vector, G, vector.tensor_scalar_mul(cl[:, 0:1], cl[:, 6:7], 1.0 / 127.0))
                    vector.tensor_copy(g["dq1"][:, tc:tc + 1], cl[:, 0:1])
                    c.guard(vector, G, vector.reciprocal(cl[:, 7:8], cl[:, 6:7]))
                    c.guard(vector, G, vector.tensor_scalar_mul(cl[:, 7:8], cl[:, 7:8], 127.0))
                    vector.tensor_scalar(q16[:], xln[tc % 2][:], cl[:, 7:8], 1536.0,
                                         ALU.mult, ALU.add)
                    vector.tensor_scalar_add(iqb[tc % 2][:], q16[:], -1536.0
                                             ).then_inc(c.sem(P + "av"), 1)

            @block.vector
            def _(vector):
                vector.wait_ge(c.sem("p0f"), 1)
                vector.tensor_scalar_mul(g["acol"][:], g["acol"][:], -1.0)
                vector.wait_ge(c.sem("p0f"), NTILES + 2)
                vector.reduce_sum(col[:, 0:1], acc[:, 0:NT1], AX)
                vector.reduce_sum(col[:, 1:2], acc[:, NT1:NTILES], AX
                                  ).then_inc(c.sem("p0m"), 1)

            @block.tensor
            def _(tensor):
                tensor.wait_ge(c.sem("p0m"), 1)
                tensor.matmul(pscl[0:1, 0:1], col[:, 0:1], onec[:], start=True, stop=True)
                tensor.matmul(pscl[0:1, 1:2], col[:, 1:2], onec[:], start=True, stop=True
                              ).then_inc(c.sem("p0m"), 1)

            @block.vector
            def _(vector):
                vector.wait_ge(c.sem("p0m"), 2)
                c.guard(vector, "gv", vector.tensor_scalar(
                    col[0:1, 2:4], pscl[0:1, 0:2],
                    1.0 / (D * DFF), 1e-5, ALU.mult, ALU.max))
                vector.reciprocal(col[0:1, 4:6], col[0:1, 2:4]).then_inc(c.sem("p0m"), 1)

            @block.tensor
            def _(tensor):
                tensor.wait_ge(c.sem("p0m"), 3)
                tensor.matmul(pscl[:, 8:12], g["ones1"][:], col[0:1, 2:6],
                              start=True, stop=True).then_inc(c.sem("p0m"), 1)

            @block.vector
            def _(vector):
                vector.wait_ge(c.sem("p0m"), 4)
                vector.tensor_copy(g["rw1"][:], pscl[:, 8:9])
                vector.tensor_copy(g["rw2"][:], pscl[:, 9:10])
                vector.tensor_copy(g["sw1"][:], pscl[:, 10:11])
                vector.tensor_copy(g["sw2"][:], pscl[:, 11:12])
                vector.tensor_scalar(g["dq1"][:, 0:NTC], g["dq1"][:, 0:NTC],
                                     g["rw1"][:, 0:1], 0.0, ALU.mult, ALU.add)


def _emit_ffn(c, call, io, g, scr):
    nc = c.nc
    x_src, x_dst = io["x_src"], io["x_dst"]
    P = f"f{call}"

    if call != 1:
        _emit_ffn_a(c, call, io, g, scr)
    _emit_ffn_rest(c, call, io, g, scr)


def _emit_ffn_a(c, call, io, g, scr):
    nc = c.nc
    x_src = io["x_src"]
    P = f"f{call}"
    with contextlib.ExitStack() as ps:
        xa = [ps.enter_context(nc.sbuf_tensor(f"{P}a_x{i}", [128, D], F32)) for i in range(2)]
        xln = [ps.enter_context(nc.sbuf_tensor(f"{P}a_ln{i}", [128, D], F32)) for i in range(2)]
        sqd = ps.enter_context(nc.sbuf_tensor(f"{P}a_sqd", [128, D], F32))
        q16 = ps.enter_context(nc.sbuf_tensor(f"{P}a_q16", [128, D], F16))
        iqb = [ps.enter_context(nc.sbuf_tensor(f"{P}a_iq{i}", [128, D], BF16)) for i in range(2)]
        cl = ps.enter_context(nc.sbuf_tensor(f"{P}a_col", [128, 8], F32))
        with nc.Block(f"{P}a") as block:
            @block.sync
            def _(sync):
                for tc in range(NTC):
                    if tc >= 2:
                        sync.wait_ge(c.sem(P + "av"), 3 * (tc - 2) + 3)
                    sync.dma_start(xa[tc % 2][:], rows_src(x_src, tc * 128, 128, D, D)
                                   ).then_inc(c.sem(P + "ax" + str(tc % 2)), 16)

            @block.gpsimd
            def _(gp):
                for tc in range(NTC):
                    gp.wait_ge(c.sem(P + "av"), 3 * tc + 3)
                    gp.dma_start(rows_src(scr["iq"], tc * 128, 128, D, D), iqb[tc % 2][:]
                                 ).then_inc(c.sem(P + "ast"), 16)

            @block.scalar
            def _(scalar):
                for tc in range(NTC):
                    scalar.wait_ge(c.sem(P + "ax" + str(tc % 2)), 16 * (tc // 2 + 1))
                    scalar.activation(sqd[:], xa[tc % 2][:], AF.Square
                                      ).then_inc(c.sem(P + "as"), 1)
                    scalar.wait_ge(c.sem(P + "av"), 3 * tc + 1)
                    scalar.activation(cl[:, 4:5], cl[:, 3:4], AF.Sqrt,
                                      bias=g["ceps"][:, 0:1]).then_inc(c.sem(P + "as"), 1)
                    scalar.wait_ge(c.sem(P + "av"), 3 * tc + 2)
                    scalar.activation(xln[tc % 2][:], xa[tc % 2][:], AF.Identity,
                                      scale=cl[:, 4:5], bias=cl[:, 5:6]
                                      ).then_inc(c.sem(P + "as"), 1)

            @block.vector
            def _(vector):
                G = "gv"
                for tc in range(NTC):
                    vector.wait_ge(c.sem(P + "ax" + str(tc % 2)), 16 * (tc // 2 + 1))
                    c.guard(vector, G, vector.reduce_sum(cl[:, 0:1], xa[tc % 2][:], AX))
                    vector.wait_ge(c.sem(P + "as"), 3 * tc + 1)
                    c.guard(vector, G, vector.reduce_sum(cl[:, 1:2], sqd[:], AX))
                    c.guard(vector, G, vector.tensor_scalar_mul(cl[:, 2:3], cl[:, 0:1], 1.0 / D))
                    c.guard(vector, G, vector.tensor_scalar_mul(cl[:, 3:4], cl[:, 1:2], 1.0 / D))
                    c.guard(vector, G, vector.tensor_mul(cl[:, 0:1], cl[:, 2:3], cl[:, 2:3]))
                    vector.tensor_sub(cl[:, 3:4], cl[:, 3:4], cl[:, 0:1]
                                      ).then_inc(c.sem(P + "av"), 1)
                    vector.wait_ge(c.sem(P + "as"), 3 * tc + 2)
                    c.guard(vector, G, vector.reciprocal(cl[:, 4:5], cl[:, 4:5]))
                    c.guard(vector, G, vector.tensor_mul(cl[:, 5:6], cl[:, 2:3], cl[:, 4:5]))
                    vector.tensor_scalar_mul(cl[:, 5:6], cl[:, 5:6], -1.0
                                             ).then_inc(c.sem(P + "av"), 1)
                    vector.wait_ge(c.sem(P + "as"), 3 * tc + 3)
                    if tc >= 2:
                        vector.wait_ge(c.sem(P + "ast"), 16 * (tc - 1))
                    c.guard(vector, G, vector.reduce_max(cl[:, 6:7], xln[tc % 2][:], AX,
                                                         apply_absolute_value=True))
                    c.guard(vector, G, vector.tensor_scalar_max(cl[:, 6:7], cl[:, 6:7], 1e-5))
                    c.guard(vector, G, vector.tensor_scalar_mul(cl[:, 0:1], cl[:, 6:7], 1.0 / 127.0))
                    vector.tensor_mul(g["dq1"][:, tc:tc + 1], cl[:, 0:1], g["rw1"][:, 0:1])
                    c.guard(vector, G, vector.reciprocal(cl[:, 7:8], cl[:, 6:7]))
                    c.guard(vector, G, vector.tensor_scalar_mul(cl[:, 7:8], cl[:, 7:8], 127.0))
                    vector.tensor_scalar(q16[:], xln[tc % 2][:], cl[:, 7:8], 1536.0,
                                         ALU.mult, ALU.add)
                    vector.tensor_scalar_add(iqb[tc % 2][:], q16[:], -1536.0
                                             ).then_inc(c.sem(P + "av"), 1)


def _emit_ffn_rest(c, call, io, g, scr):
    nc = c.nc
    x_src, x_dst = io["x_src"], io["x_dst"]
    P = f"f{call}"
    with contextlib.ExitStack() as ps:
        iqf = ps.enter_context(nc.sbuf_tensor(f"{P}c_iqf", [128, NDC * T], BF16))
        iw1 = [ps.enter_context(nc.sbuf_tensor(f"{P}c_iw1{i}", [128, NDC * 512], BF16))
               for i in range(2)]
        if call == 1:
            w1b = [ps.enter_context(nc.sbuf_tensor(f"{P}c_w1{i}", [128, NDC * 512], F32))
                   for i in range(2)]
            q16w = [ps.enter_context(nc.sbuf_tensor(f"{P}c_q16w{i}", [128, 512], F16))
                    for i in range(2)]
        hbuf = [ps.enter_context(nc.sbuf_tensor(f"{P}c_h{i}", [128, 512], F32)) for i in range(4)]
        pmm = [ps.enter_context(nc.psum_tensor(f"{P}c_ps{i}", [128, 512], F32)) for i in range(4)]

        with nc.Block(f"{P}c") as block:
            def ld_w1(sync, fg):
                if call == 1:
                    if fg >= 2:
                        sync.wait_ge(c.sem(P + "cq"), fg - 1)
                    for h in range(2):
                        sync.dma_start(w1b[fg % 2][:, h * 8 * 512:(h + 1) * 8 * 512],
                                       chunked_src(io["w1t"], h * 1024, 1024, fg * 512, 512, DFF)
                                       ).then_inc(c.sem(P + "cw" + str(fg % 2)), 16)
                else:
                    if fg >= 2:
                        sync.wait_ge(c.sem(P + "cmm"), (fg - 1) * NTC)
                    for h in range(2):
                        sync.dma_start(iw1[fg % 2][:, h * 8 * 512:(h + 1) * 8 * 512],
                                       chunked_src(scr["iw1"], h * 1024, 1024, fg * 512, 512, DFF)
                                       ).then_inc(c.sem(P + "cw" + str(fg % 2)), 16)

            @block.sync
            def _(sync):
                sync.wait_ge(c.sem(P + "ast"), 16 * NTC)
                ld_w1(sync, 0)
                ld_w1(sync, 1)
                for dc in range(NDC):
                    sync.dma_start(iqf[:, dc * T:(dc + 1) * T],
                                   bass.AP(scr["iq"].tensor, dc * 128, [[D, T], [1, 128]]),
                                   transpose=True).then_inc(c.sem(P + "ctl"), 16)
                for fg in range(2, NFG):
                    ld_w1(sync, fg)

            @block.gpsimd
            def _(gp):
                for fg in range(NFG):
                    if call == 1:
                        gp.wait_ge(c.sem(P + "cq"), fg + 1)
                        gp.dma_start(chunked_src(scr["iw1"], 0, D, fg * 512, 512, DFF),
                                     iw1[fg % 2][:]).then_inc(c.sem(P + "csp"), 16)
                    for tc in range(NTC):
                        k = fg * NTC + tc
                        gp.wait_ge(c.sem(P + "cev"), k + 1)
                        gp.dma_start(rows_src(scr["h"], tc * 128, 128, 512, DFF, fg * 512),
                                     hbuf[k % 4][:]).then_inc(c.sem(P + "cst"), 16)

            def _emit_evac_sc(scalar, fg):
                for tc in range(NTC):
                    k = fg * NTC + tc
                    scalar.wait_ge(c.sem(P + "cmm"), k + 1)
                    if k >= 4:
                        scalar.wait_ge(c.sem(P + "cst"), 16 * (k - 3))
                        scalar.wait_ge(c.sem(P + "crm"), k - 3)
                    scalar.activation(hbuf[k % 4][:], pmm[k % 4][:], AF.Silu,
                                      scale=g["dq1"][:, tc:tc + 1]
                                      ).then_inc(c.sem(P + "cev"), 1)

            def _emit_rm_v(vector, fg):
                for tc in range(NTC):
                    k = fg * NTC + tc
                    vector.wait_ge(c.sem(P + "cev"), k + 1)
                    vector.reduce_max(g["hmax"][:, tc * NFG + fg: tc * NFG + fg + 1],
                                      hbuf[k % 4][:], AX, apply_absolute_value=True
                                      ).then_inc(c.sem(P + "crm"), 1)

            if call == 1:
                @block.scalar
                def _(scalar):
                    for fg in range(NFG):
                        if fg >= 1:
                            _emit_evac_sc(scalar, fg - 1)
                        scalar.wait_ge(c.sem(P + "cw" + str(fg % 2)), 32 * (fg // 2 + 1))
                        for dc in range(NDC):
                            idx = fg * NDC + dc
                            if idx >= 2:
                                scalar.wait_ge(c.sem(P + "cqv"), idx - 1)
                            scalar.activation(q16w[dc % 2][:],
                                              w1b[fg % 2][:, dc * 512:(dc + 1) * 512],
                                              AF.Identity, scale=g["sw1"][:, 0:1],
                                              bias=g["c1536"][:, 0:1]
                                              ).then_inc(c.sem(P + "cqs"), 1)
                    _emit_evac_sc(scalar, NFG - 1)

                @block.vector
                def _(vector):
                    for fg in range(NFG):
                        if fg >= 1:
                            _emit_rm_v(vector, fg - 1)
                        if fg >= 2:
                            vector.wait_ge(c.sem(P + "csp"), 16 * (fg - 1))
                            vector.wait_ge(c.sem(P + "cmm"), (fg - 1) * NTC)
                        for dc in range(NDC):
                            vector.wait_ge(c.sem(P + "cqs"), fg * NDC + dc + 1)
                            sl = iw1[fg % 2][:, dc * 512:(dc + 1) * 512]
                            vector.tensor_scalar(sl, q16w[dc % 2][:], -1536.0, 1.0,
                                                 ALU.add, ALU.min)
                            vector.tensor_scalar_max(sl, sl, -1.0
                                                     ).then_inc(c.sem(P + "cqv"), 1)
                            if dc == NDC - 1:
                                vector.nop().then_inc(c.sem(P + "cq"), 1)
                    _emit_rm_v(vector, NFG - 1)
            else:
                @block.scalar
                def _(scalar):
                    for fg in range(NFG):
                        _emit_evac_sc(scalar, fg)

                @block.vector
                def _(vector):
                    for fg in range(NFG):
                        _emit_rm_v(vector, fg)

            @block.tensor
            def _(tensor):
                tensor.wait_ge(c.sem(P + "ctl"), 16 * NDC)
                for fg in range(NFG):
                    if call == 1:
                        tensor.wait_ge(c.sem(P + "cq"), fg + 1)
                    else:
                        tensor.wait_ge(c.sem(P + "cw" + str(fg % 2)), 32 * (fg // 2 + 1))
                    for tc in range(NTC):
                        k = fg * NTC + tc
                        if k >= 4:
                            tensor.wait_ge(c.sem(P + "cev"), k - 3)
                        for dc in range(NDC):
                            ins = tensor.matmul(
                                pmm[k % 4][:],
                                iqf[:, dc * T + tc * 128: dc * T + (tc + 1) * 128],
                                iw1[fg % 2][:, dc * 512:(dc + 1) * 512],
                                start=(dc == 0), stop=(dc == NDC - 1))
                        ins.then_inc(c.sem(P + "cmm"), 1)

    with contextlib.ExitStack() as ps:
        hcb = [ps.enter_context(nc.sbuf_tensor(f"{P}e_h{i}", [128, DFF], F32)) for i in range(2)]
        q16h = ps.enter_context(nc.sbuf_tensor(f"{P}e_q16", [128, DFF], F16))
        ihb = [ps.enter_context(nc.sbuf_tensor(f"{P}e_ih{i}", [128, DFF], BF16)) for i in range(2)]
        cl2 = ps.enter_context(nc.sbuf_tensor(f"{P}e_col", [128, 4], F32))
        with nc.Block(f"{P}e") as block:
            @block.sync
            def _(sync):
                sync.wait_ge(c.sem(P + "cst"), 16 * NFG * NTC)
                for tc in range(NTC):
                    if tc >= 2:
                        sync.wait_ge(c.sem(P + "dv"), tc - 1)
                    for h in range(2):
                        sync.dma_start(hcb[tc % 2][:, h * (DFF // 2):(h + 1) * (DFF // 2)],
                                       rows_src(scr["h"], tc * 128, 128, DFF // 2, DFF,
                                                h * (DFF // 2))
                                       ).then_inc(c.sem(P + "dh" + str(tc % 2)), 16)

            @block.vector
            def _(vector):
                G = "gv"
                for tc in range(NTC):
                    vector.wait_ge(c.sem(P + "dh" + str(tc % 2)), 32 * (tc // 2 + 1))
                    if tc >= 2:
                        vector.wait_ge(c.sem(P + "dst"), 16 * (tc - 1))
                    c.guard(vector, G, vector.reduce_max(
                        cl2[:, 0:1], g["hmax"][:, tc * NFG:(tc + 1) * NFG], AX))
                    c.guard(vector, G, vector.tensor_scalar_max(cl2[:, 0:1], cl2[:, 0:1], 1e-5))
                    c.guard(vector, G, vector.tensor_scalar_mul(cl2[:, 1:2], cl2[:, 0:1], 0.5 / 127.0))
                    vector.tensor_mul(g["dq2h"][:, tc:tc + 1], cl2[:, 1:2], g["rw2"][:, 0:1])
                    c.guard(vector, G, vector.reciprocal(cl2[:, 2:3], cl2[:, 0:1]))
                    c.guard(vector, G, vector.tensor_scalar_mul(cl2[:, 2:3], cl2[:, 2:3], 127.0))
                    vector.tensor_scalar(q16h[:], hcb[tc % 2][:], cl2[:, 2:3], 1536.0,
                                         ALU.mult, ALU.add)
                    vector.tensor_scalar_add(ihb[tc % 2][:], q16h[:], -1536.0
                                             ).then_inc(c.sem(P + "dv"), 1)

            @block.gpsimd
            def _(gp):
                for tc in range(NTC):
                    gp.wait_ge(c.sem(P + "dv"), tc + 1)
                    gp.dma_start(rows_src(scr["ih"], tc * 128, 128, DFF, DFF), ihb[tc % 2][:]
                                 ).then_inc(c.sem(P + "dst"), 16)

    NGRP = 2
    TCG = NTC // NGRP
    TL = T // NGRP
    NFCG = 8
    with contextlib.ExitStack() as ps:
        ihf = [ps.enter_context(nc.sbuf_tensor(f"{P}g_ihf", [128, NFC * TL], BF16))] * 2
        x0g = ps.enter_context(nc.sbuf_tensor(f"{P}g_x0", [128, TCG * 512], F32))
        ob = [ps.enter_context(nc.sbuf_tensor(f"{P}g_o{i}", [128, 512], F32)) for i in range(4)]
        iw2 = [ps.enter_context(nc.sbuf_tensor(f"{P}g_iw2{i}", [128, 8 * 512], BF16))
               for i in range(2)]
        if call == 1:
            w2b = [ps.enter_context(nc.sbuf_tensor(f"{P}g_w2{i}", [128, 8 * 512], F32))
                   for i in range(2)]
            q16w2 = [ps.enter_context(nc.sbuf_tensor(f"{P}g_q16w{i}", [128, 512], F16))
                     for i in range(2)]
        pdm = [ps.enter_context(nc.psum_tensor(f"{P}g_ps{i}", [128, 512], F32))
               for i in range(TCG)]

        with nc.Block(f"{P}g") as block:
            @block.sync
            def _(sync):
                for grp in range(NGRP):
                    sync.wait_ge(c.sem(P + "dst"), 16 * (grp + 1) * TCG)
                    if grp > 0:
                        sync.wait_ge(c.sem(P + "gmm"), grp * 4 * NFCG)
                    for fc in range(NFC):
                        sync.dma_start(ihf[grp % 2][:, fc * TL:(fc + 1) * TL],
                                       bass.AP(scr["ih"].tensor, grp * TL * DFF + fc * 128,
                                               [[DFF, TL], [1, 128]]),
                                       transpose=True).then_inc(c.sem(P + "gtl"), 16)
                    for dg in range(4):
                        for fcg in range(NFCG):
                            j = (grp * 4 + dg) * NFCG + fcg
                            if call == 1 and grp == 0:
                                if j >= 2:
                                    sync.wait_ge(c.sem(P + "gq"), j - 1)
                                for h in range(2):
                                    sync.dma_start(w2b[j % 2][:, h * 4 * 512:(h + 1) * 4 * 512],
                                                   chunked_src(io["w2t"], fcg * 1024 + h * 512, 512,
                                                               dg * 512, 512, D)
                                                   ).then_inc(c.sem(P + "gw" + str(j % 2)), 16)
                            else:
                                if j >= 2:
                                    sync.wait_ge(c.sem(P + "gmm"), j - 1)
                                if call == 1:
                                    sync.wait_ge(c.sem(P + "gsp"),
                                                 16 * (dg * NFCG + fcg + 1))
                                for h in range(2):
                                    sync.dma_start(iw2[j % 2][:, h * 4 * 512:(h + 1) * 4 * 512],
                                                   chunked_src(scr["iw2"], fcg * 1024 + h * 512, 512,
                                                               dg * 512, 512, D)
                                                   ).then_inc(c.sem(P + "gw" + str(j % 2)), 16)
                        gd = grp * 4 + dg
                        if gd >= 1:
                            sync.wait_ge(c.sem(P + "gev"), gd * TCG)
                        sync.dma_start(x0g[:],
                                       chunked_src(x_src, grp * TCG * 128, TCG * 128,
                                                   dg * 512, 512, D)
                                       ).then_inc(c.sem(P + "gx"), 16)

            def _emit_gevac_v(vector, grp, dg):
                gd = grp * 4 + dg
                for i in range(TCG):
                    tc = grp * TCG + i
                    ke = gd * TCG + i
                    vector.wait_ge(c.sem(P + "gmm"), (gd + 1) * NFCG)
                    vector.wait_ge(c.sem(P + "gx"), 16 * (gd + 1))
                    if ke >= 4:
                        vector.wait_ge(c.sem(P + "gost"), 16 * (ke - 3))
                    vector.scalar_tensor_tensor(
                        ob[ke % 4][:], pdm[i][:], g["dq2h"][:, tc:tc + 1],
                        x0g[:, i * 512:(i + 1) * 512],
                        ALU.mult, ALU.add).then_inc(c.sem(P + "gev"), 1)

            if call == 1:
                @block.scalar
                def _(scalar):
                    for dg in range(4):
                        for fcg in range(NFCG):
                            j = dg * NFCG + fcg
                            scalar.wait_ge(c.sem(P + "gw" + str(j % 2)), 32 * (j // 2 + 1))
                            for cc in range(8):
                                idx = j * 8 + cc
                                if idx >= 2:
                                    scalar.wait_ge(c.sem(P + "gqv"), idx - 1)
                                scalar.activation(q16w2[cc % 2][:],
                                                  w2b[j % 2][:, cc * 512:(cc + 1) * 512],
                                                  AF.Identity, scale=g["sw2"][:, 0:1],
                                                  bias=g["c1536"][:, 0:1]
                                                  ).then_inc(c.sem(P + "gqs"), 1)

                @block.vector
                def _(vector):
                    for dg in range(4):
                        if dg >= 1:
                            _emit_gevac_v(vector, 0, dg - 1)
                        for fcg in range(NFCG):
                            j = dg * NFCG + fcg
                            if j >= 2:
                                vector.wait_ge(c.sem(P + "gsp"), 16 * (j - 1))
                                vector.wait_ge(c.sem(P + "gmm"), j - 1)
                            for cc in range(8):
                                vector.wait_ge(c.sem(P + "gqs"), j * 8 + cc + 1)
                                sl = iw2[j % 2][:, cc * 512:(cc + 1) * 512]
                                vector.tensor_scalar(sl, q16w2[cc % 2][:], -1536.0, 1.0,
                                                     ALU.add, ALU.min)
                                vector.tensor_scalar_max(sl, sl, -1.0
                                                         ).then_inc(c.sem(P + "gqv"), 1)
                                if cc == 7:
                                    vector.nop().then_inc(c.sem(P + "gq"), 1)
                    _emit_gevac_v(vector, 0, 3)
                    for grp in range(1, NGRP):
                        for dg in range(4):
                            _emit_gevac_v(vector, grp, dg)

                @block.gpsimd
                def _(gp):
                    def _stores(grp, dg):
                        for i in range(TCG):
                            tc = grp * TCG + i
                            ke = (grp * 4 + dg) * TCG + i
                            gp.wait_ge(c.sem(P + "gev"), ke + 1)
                            gp.dma_start(rows_src(x_dst, tc * 128, 128, 512, D, dg * 512),
                                         ob[ke % 4][:]).then_inc(c.sem(P + "gost"), 16)
                    for dg in range(4):
                        if dg >= 1:
                            _stores(0, dg - 1)
                        for fcg in range(NFCG):
                            j = dg * NFCG + fcg
                            gp.wait_ge(c.sem(P + "gq"), j + 1)
                            gp.dma_start(chunked_src(scr["iw2"], fcg * 1024, 1024,
                                                     dg * 512, 512, D),
                                         iw2[j % 2][:]).then_inc(c.sem(P + "gsp"), 16)
                    _stores(0, 3)
                    for grp in range(1, NGRP):
                        for dg in range(4):
                            _stores(grp, dg)
            else:
                @block.vector
                def _(vector):
                    for grp in range(NGRP):
                        for dg in range(4):
                            _emit_gevac_v(vector, grp, dg)

            @block.tensor
            def _(tensor):
                for grp in range(NGRP):
                    tensor.wait_ge(c.sem(P + "gtl"), 16 * (grp + 1) * NFC)
                    for dg in range(4):
                        gd = grp * 4 + dg
                        for fcg in range(NFCG):
                            j = gd * NFCG + fcg
                            if call == 1 and grp == 0:
                                tensor.wait_ge(c.sem(P + "gq"), j + 1)
                            else:
                                tensor.wait_ge(c.sem(P + "gw" + str(j % 2)), 32 * (j // 2 + 1))
                            for i in range(TCG):
                                if fcg == 0 and gd >= 1:
                                    tensor.wait_ge(c.sem(P + "gev"), (gd - 1) * TCG + i + 1)
                                for fi in range(8):
                                    fc = fcg * 8 + fi
                                    ins = tensor.matmul(
                                        pdm[i][:],
                                        ihf[grp % 2][:, fc * TL + i * 128: fc * TL + (i + 1) * 128],
                                        iw2[j % 2][:, fi * 512:(fi + 1) * 512],
                                        start=(fcg == 0 and fi == 0),
                                        stop=(fcg == NFCG - 1 and fi == 7))
                            ins.then_inc(c.sem(P + "gmm"), 1)

            if call != 1:
                @block.gpsimd
                def _(gp):
                    for grp in range(NGRP):
                        for dg in range(4):
                            for i in range(TCG):
                                tc = grp * TCG + i
                                ke = (grp * 4 + dg) * TCG + i
                                gp.wait_ge(c.sem(P + "gev"), ke + 1)
                                gp.dma_start(rows_src(x_dst, tc * 128, 128, 512, D, dg * 512),
                                             ob[ke % 4][:]).then_inc(c.sem(P + "gost"), 16)
    c.cnt[P + "gost"] = 16 * NGRP * 4 * TCG



def _emit_mamba(c, io, g, scr):
    nc = c.nc
    x_src, x_dst = io["x_src"], io["x_dst"]
    P = "m"
    TCH = 256
    NCH = T // TCH
    NTC2 = TCH // 128

    with contextlib.ExitStack() as outer:

        with contextlib.ExitStack() as ps:
            xfh = ps.enter_context(nc.sbuf_tensor("ma_xfh", [128, NDC * T], BF16))
            xfl = ps.enter_context(nc.sbuf_tensor("ma_xfl", [128, NDC * T], BF16))
            _emit_mamba_a(c, g, x_src, xfh, xfl)
            _emit_mamba_b(c, g, io, scr, xfh, xfl)
        _emit_mamba_c(c, g, io, scr)
        _emit_mamba_d(c, g, io, scr)
        _emit_mamba_d2(c, g, io, scr)
        _emit_mamba_e(c, g, io, scr, x_src, x_dst, TCH, NCH, NTC2)
        c.cnt["meost"] = 16 * NCH * 4 * NTC2


def _emit_mamba_a(c, g, x_src, xfh, xfl):
    nc = c.nc
    P = "ma"
    with contextlib.ExitStack() as ps:
        xa = [ps.enter_context(nc.sbuf_tensor(f"ma_x{i}", [128, D], F32)) for i in range(2)]
        sqd = ps.enter_context(nc.sbuf_tensor("ma_sqd", [128, D], F32))
        cl = ps.enter_context(nc.sbuf_tensor("ma_col", [128, 8], F32))
        diag = [ps.enter_context(nc.sbuf_tensor(f"ma_dg{i}", [128, 128], F32)) for i in range(2)]
        nmr = [ps.enter_context(nc.sbuf_tensor(f"ma_nm{i}", [1, 128], F32)) for i in range(2)]
        pt = [ps.enter_context(nc.psum_tensor(f"ma_pt{i}", [128, 128], F32)) for i in range(4)]
        pr = ps.enter_context(nc.psum_tensor("ma_pr", [1, 128], F32))
        with nc.Block("ma") as block:
            @block.sync
            def _(sync):
                for tc in range(NTC):
                    if tc >= 2:
                        sync.wait_ge(c.sem(P + "ev"), 16 * (tc - 1))
                    sync.dma_start(xa[tc % 2][:], rows_src(x_src, tc * 128, 128, D, D)
                                   ).then_inc(c.sem(P + "x"), 16)

            @block.scalar
            def _(scalar):
                for tc in range(NTC):
                    scalar.wait_ge(c.sem(P + "x"), 16 * (tc + 1))
                    scalar.activation(sqd[:], xa[tc % 2][:], AF.Square
                                      ).then_inc(c.sem(P + "s"), 1)
                    scalar.wait_ge(c.sem(P + "v"), 3 * tc + 1)
                    scalar.activation(cl[:, 4:5], cl[:, 3:4], AF.Sqrt,
                                      bias=g["ceps"][:, 0:1]).then_inc(c.sem(P + "s"), 1)

            def _ma_evacs(vector, tc):
                for dc in range(NDC):
                    vector.wait_ge(c.sem(P + "tm"), tc * 17 + 1 + dc + 1)
                    dst_h = xfh[:, dc * T + tc * 128: dc * T + (tc + 1) * 128]
                    dst_l = xfl[:, dc * T + tc * 128: dc * T + (tc + 1) * 128]
                    vector.tensor_copy(dst_h, pt[dc % 4][:])
                    vector.tensor_sub(dst_l, pt[dc % 4][:], dst_h
                                      ).then_inc(c.sem(P + "ev"), 1)

            @block.vector
            def _(vector):
                G = "gv"
                for tc in range(NTC):
                    vector.wait_ge(c.sem(P + "x"), 16 * (tc + 1))
                    c.guard(vector, G, vector.reduce_sum(cl[:, 0:1], xa[tc % 2][:], AX))
                    vector.wait_ge(c.sem(P + "s"), 2 * tc + 1)
                    c.guard(vector, G, vector.reduce_sum(cl[:, 1:2], sqd[:], AX))
                    c.guard(vector, G, vector.tensor_scalar_mul(cl[:, 2:3], cl[:, 0:1], 1.0 / D))
                    c.guard(vector, G, vector.tensor_scalar_mul(cl[:, 3:4], cl[:, 1:2], 1.0 / D))
                    c.guard(vector, G, vector.tensor_mul(cl[:, 0:1], cl[:, 2:3], cl[:, 2:3]))
                    vector.tensor_sub(cl[:, 3:4], cl[:, 3:4], cl[:, 0:1]
                                      ).then_inc(c.sem(P + "v"), 1)
                    vector.wait_ge(c.sem(P + "s"), 2 * tc + 2)
                    c.guard(vector, G, vector.reciprocal(cl[:, 4:5], cl[:, 4:5]))
                    c.guard(vector, G, vector.tensor_mul(cl[:, 5:6], cl[:, 2:3], cl[:, 4:5]))
                    c.guard(vector, G, vector.tensor_scalar_mul(cl[:, 5:6], cl[:, 5:6], -1.0))
                    vector.tensor_scalar_mul(diag[tc % 2][:], g["eye"][:], cl[:, 4:5]
                                             ).then_inc(c.sem(P + "v"), 1)
                    if tc >= 1:
                        _ma_evacs(vector, tc - 1)
                    vector.wait_ge(c.sem(P + "tm"), tc * 17 + 1)
                    vector.tensor_copy(nmr[tc % 2][:], pr[0:1, :]).then_inc(c.sem(P + "v"), 1)
                _ma_evacs(vector, NTC - 1)

            @block.tensor
            def _(tensor):
                for tc in range(NTC):
                    tensor.wait_ge(c.sem(P + "v"), 3 * tc + 2)
                    tensor.transpose(pr[:], cl[:, 5:6], g["eye"][:]).then_inc(c.sem(P + "tm"), 1)
                    tensor.wait_ge(c.sem(P + "v"), 3 * tc + 3)
                    for dc in range(NDC):
                        gi = tc * NDC + dc
                        if gi >= 4:
                            tensor.wait_ge(c.sem(P + "ev"), gi - 3)
                        tensor.matmul(pt[dc % 4][:], xa[tc % 2][:, dc * 128:(dc + 1) * 128],
                                      diag[tc % 2][:], start=True, stop=False)
                        tensor.matmul(pt[dc % 4][:], g["ones1"][:], nmr[tc % 2][:],
                                      start=False, stop=True).then_inc(c.sem(P + "tm"), 1)


def _emit_mamba_b(c, g, io, scr, xfh, xfl):
    nc = c.nc
    P = "mb"
    NOC = (2 * DIN) // 128
    with contextlib.ExitStack() as ps:
        wb = [ps.enter_context(nc.sbuf_tensor(f"mb_w{i}", [128, NDC * 128], F32)) for i in range(2)]
        wh = [ps.enter_context(nc.sbuf_tensor(f"mb_wh{i}", [128, NDC * 128], BF16)) for i in range(2)]
        wl = [ps.enter_context(nc.sbuf_tensor(f"mb_wl{i}", [128, NDC * 128], BF16)) for i in range(2)]
        evh = [ps.enter_context(nc.sbuf_tensor(f"mb_eh{i}", [128, 512], BF16)) for i in range(4)]
        evl = [ps.enter_context(nc.sbuf_tensor(f"mb_el{i}", [128, 512], BF16)) for i in range(4)]
        evs = [ps.enter_context(nc.sbuf_tensor(f"mb_es{i}", [128, 512], F32)) for i in range(4)]
        pb = [ps.enter_context(nc.psum_tensor(f"mb_ps{i}", [128, 512], F32)) for i in range(2)]
        with nc.Block("mb") as block:
            @block.sync
            def _(sync):
                for oc in range(NOC):
                    if oc >= 2:
                        sync.wait_ge(c.sem(P + "q"), oc - 1)
                    sync.dma_start(wb[oc % 2][:],
                                   chunked_src(io["w_in"], 0, D, oc * 128, 128, 2 * DIN)
                                   ).then_inc(c.sem(P + "w"), 16)

            @block.vector
            def _(vector):
                for oc in range(NOC):
                    vector.wait_ge(c.sem(P + "w"), 16 * (oc + 1))
                    if oc >= 2:
                        vector.wait_ge(c.sem(P + "mm"), (oc - 1) * 4)
                    vector.tensor_copy(wh[oc % 2][:], wb[oc % 2][:])
                    vector.tensor_sub(wl[oc % 2][:], wb[oc % 2][:], wh[oc % 2][:]
                                      ).then_inc(c.sem(P + "q"), 1)
                    if oc >= 1:
                        _mb_evac_v(c, vector, oc - 1, pb, evh, evl)
                _mb_evac_v(c, vector, NOC - 1, pb, evh, evl)

            @block.scalar
            def _(scalar):
                for oc in range(16, NOC):
                    for tg in range(4):
                        k = oc * 4 + tg
                        kr = (oc - 16) * 4 + tg
                        scalar.wait_ge(c.sem(P + "mm"), k + 1)
                        if kr >= 4:
                            scalar.wait_ge(c.sem(P + "st"), 16 * (128 + kr - 3))
                        scalar.activation(evs[tg % 4][:], pb[k % 2][:], AF.Silu
                                          ).then_inc(c.sem(P + "evs"), 1)

            @block.tensor
            def _(tensor):
                for oc in range(NOC):
                    tensor.wait_ge(c.sem(P + "q"), oc + 1)
                    for tg in range(4):
                        k = oc * 4 + tg
                        if k >= 2:
                            tensor.wait_ge(c.sem(P + "ev"), k - 1)
                        for dc in range(NDC):
                            for (lt, rt) in ((wh, xfh), (wh, xfl), (wl, xfh)):
                                ins = tensor.matmul(
                                    pb[k % 2][:],
                                    lt[oc % 2][:, dc * 128:(dc + 1) * 128],
                                    rt[:, dc * T + tg * 512: dc * T + tg * 512 + 512],
                                    start=(dc == 0 and lt is wh and rt is xfh),
                                    stop=(dc == NDC - 1 and lt is wl))
                        ins.then_inc(c.sem(P + "mm"), 1)

            @block.gpsimd
            def _(gp):
                for oc in range(NOC):
                    for tg in range(4):
                        k = oc * 4 + tg
                        if oc < 16:
                            gp.wait_ge(c.sem(P + "ev"), k + 1)
                            gp.dma_start(rows_src(scr["xch"], oc * 128, 128, 512, T, tg * 512),
                                         evh[k % 4][:]).then_inc(c.sem(P + "st"), 16)
                            gp.dma_start(rows_src(scr["xcl"], oc * 128, 128, 512, T, tg * 512),
                                         evl[k % 4][:]).then_inc(c.sem(P + "st"), 16)
                        else:
                            gp.wait_ge(c.sem(P + "evs"), (oc - 16) * 4 + tg + 1)
                            gp.dma_start(rows_src(scr["sr"], (oc - 16) * 128, 128, 512, T, tg * 512),
                                         evs[tg % 4][:]).then_inc(c.sem(P + "st"), 16)


def _mb_evac_v(c, vector, oc, pb, evh, evl):
    P = "mb"
    if oc >= 16:
        for tg in range(4):
            k = oc * 4 + tg
            vector.wait_ge(c.sem(P + "evs"), (oc - 16) * 4 + tg + 1)
            vector.nop().then_inc(c.sem(P + "ev"), 1)
        return
    for tg in range(4):
        k = oc * 4 + tg
        vector.wait_ge(c.sem(P + "mm"), k + 1)
        if k >= 4:
            vector.wait_ge(c.sem(P + "st"), 32 * (k - 3))
        vector.tensor_copy(evh[k % 4][:], pb[k % 2][:])
        vector.tensor_sub(evl[k % 4][:], pb[k % 2][:], evh[k % 4][:]
                          ).then_inc(c.sem(P + "ev"), 1)


def _emit_mamba_c(c, g, io, scr):
    nc = c.nc
    P = "mc"
    TP = T + 4
    with contextlib.ExitStack() as ps:
        xch = ps.enter_context(nc.sbuf_tensor("mc_xh", [128, NDC * TP], BF16))
        xcl = ps.enter_context(nc.sbuf_tensor("mc_xl", [128, NDC * TP], BF16))
        wb = [ps.enter_context(nc.sbuf_tensor(f"mc_w{i}", [128, NDC * 128], F32))
              for i in range(2)]
        whc = ps.enter_context(nc.sbuf_tensor("mc_wh", [128, 5 * NDC * 128], BF16))
        wlc = ps.enter_context(nc.sbuf_tensor("mc_wl", [128, 5 * NDC * 128], BF16))
        exs = [ps.enter_context(nc.sbuf_tensor(f"mc_xs{i}", [128, 512], F32)) for i in range(2)]
        eu = [ps.enter_context(nc.sbuf_tensor(f"mc_u{i}", [128, 512], F32)) for i in range(2)]
        euh = [ps.enter_context(nc.sbuf_tensor(f"mc_uh{i}", [128, 512], BF16)) for i in range(2)]
        eul = [ps.enter_context(nc.sbuf_tensor(f"mc_ul{i}", [128, 512], BF16)) for i in range(2)]
        pc = [ps.enter_context(nc.psum_tensor(f"mc_ps{i}", [128, 512], F32)) for i in range(2)]
        with nc.Block("mc") as block:
            @block.sync
            def _(sync):
                sync.wait_ge(c.sem("mb" + "st"), 16 * (16 * 4 * 2 + 16 * 4))
                for ic in range(NDC):
                    sync.dma_start(xch[:, ic * TP + 3: ic * TP + 3 + T],
                                   rows_src(scr["xch"], ic * 128, 128, T, T)
                                   ).then_inc(c.sem(P + "x"), 16)
                    sync.dma_start(xcl[:, ic * TP + 3: ic * TP + 3 + T],
                                   rows_src(scr["xcl"], ic * 128, 128, T, T)
                                   ).then_inc(c.sem(P + "x"), 16)
                for oc in range(NDC):
                    for k in range(KCV):
                        j = oc * KCV + k
                        if j >= 2:
                            sync.wait_ge(c.sem(P + "q"), j)
                        sync.dma_start(wb[j % 2][:],
                                       chunked_src(io["convw"], k * DIN, DIN, oc * 128, 128, DIN)
                                       ).then_inc(c.sem(P + "w" + str(j % 2)), 16)

            @block.vector
            def _(vector):
                for ic in range(NDC):
                    vector.memset(xch[:, ic * TP: ic * TP + 3], 0.0)
                    vector.memset(xcl[:, ic * TP: ic * TP + 3], 0.0)
                vector.nop().then_inc(c.sem(P + "q"), 1)
                for oc in range(NDC):
                    for k in range(KCV):
                        j = oc * KCV + k
                        vector.wait_ge(c.sem(P + "w" + str(j % 2)), 16 * (j // 2 + 1))
                        if j >= 5:
                            vector.wait_ge(c.sem(P + "mm"), ((j - 5) // KCV + 1) * 4)
                        sl_h = whc[:, (j % 5) * NDC * 128: (j % 5 + 1) * NDC * 128]
                        sl_l = wlc[:, (j % 5) * NDC * 128: (j % 5 + 1) * NDC * 128]
                        vector.tensor_copy(sl_h, wb[j % 2][:])
                        vector.tensor_sub(sl_l, wb[j % 2][:], sl_h).then_inc(c.sem(P + "q"), 1)
                    if oc >= 1:
                        _mc_usplit_v(c, vector, oc - 1, eu, euh, eul)
                _mc_usplit_v(c, vector, NDC - 1, eu, euh, eul)

            @block.tensor
            def _(tensor):
                tensor.wait_ge(c.sem(P + "x"), 16 * 2 * NDC)
                for oc in range(NDC):
                    tensor.wait_ge(c.sem(P + "q"), oc * KCV + KCV + 1)
                    for tg in range(4):
                        kk = oc * 4 + tg
                        if kk >= 2:
                            tensor.wait_ge(c.sem(P + "es"), 2 * (kk - 1))
                        first = True
                        for k in range(KCV):
                            j = oc * KCV + k
                            slot = j % 5
                            for ic in range(NDC):
                                off = ic * TP + tg * 512 + k
                                for (lt, rt) in ((whc, xch), (whc, xcl), (wlc, xch)):
                                    ins = tensor.matmul(
                                        pc[kk % 2][:],
                                        lt[:, slot * NDC * 128 + ic * 128: slot * NDC * 128 + (ic + 1) * 128],
                                        rt[:, off: off + 512],
                                        start=first, stop=(k == KCV - 1 and ic == NDC - 1 and lt is wlc))
                                    first = False
                        ins.then_inc(c.sem(P + "mm"), 1)

            @block.scalar
            def _(scalar):
                for oc in range(NDC):
                    for tg in range(4):
                        k = oc * 4 + tg
                        scalar.wait_ge(c.sem(P + "mm"), k + 1)
                        if k >= 2:
                            scalar.wait_ge(c.sem(P + "st"), 64 * (k - 1))
                            scalar.wait_ge(c.sem(P + "ev"), k - 1)
                        scalar.activation(exs[k % 2][:], pc[k % 2][:], AF.Identity,
                                          bias=g["convb"][:, oc:oc + 1])
                        scalar.activation(eu[k % 2][:], pc[k % 2][:], AF.Silu,
                                          bias=g["convb"][:, oc:oc + 1]
                                          ).then_inc(c.sem(P + "es"), 2)

            @block.gpsimd
            def _(gp):
                for oc in range(NDC):
                    for tg in range(4):
                        k = oc * 4 + tg
                        gp.wait_ge(c.sem(P + "es"), 2 * (k + 1))
                        gp.dma_start(rows_src(scr["xs"], oc * 128, 128, 512, T, tg * 512),
                                     exs[k % 2][:]).then_inc(c.sem(P + "st"), 16)
                        gp.dma_start(rows_src(scr["u"], oc * 128, 128, 512, T, tg * 512),
                                     eu[k % 2][:]).then_inc(c.sem(P + "st"), 16)
                        gp.wait_ge(c.sem(P + "ev"), k + 1)
                        gp.dma_start(rows_src(scr["uh"], oc * 128, 128, 512, T, tg * 512),
                                     euh[k % 2][:]).then_inc(c.sem(P + "st"), 16)
                        gp.dma_start(rows_src(scr["ul"], oc * 128, 128, 512, T, tg * 512),
                                     eul[k % 2][:]).then_inc(c.sem(P + "st"), 16)


def _mc_usplit_v(c, vector, oc, eu, euh, eul):
    P = "mc"
    for tg in range(4):
        k = oc * 4 + tg
        vector.wait_ge(c.sem(P + "es"), 2 * (k + 1))
        if k >= 2:
            vector.wait_ge(c.sem(P + "st"), 64 * (k - 1))
        vector.tensor_copy(euh[k % 2][:], eu[k % 2][:])
        vector.tensor_sub(eul[k % 2][:], eu[k % 2][:], euh[k % 2][:]
                          ).then_inc(c.sem(P + "ev"), 1)


def _emit_mamba_d(c, g, io, scr):
    nc = c.nc
    P = "md"
    with contextlib.ExitStack() as ps:
        bc_sb = ps.enter_context(nc.sbuf_tensor("m_bc", [32, T], F32))
        uh = ps.enter_context(nc.sbuf_tensor("md_uh", [128, NDC * T], BF16))
        ul = ps.enter_context(nc.sbuf_tensor("md_ul", [128, NDC * T], BF16))
        wxb = ps.enter_context(nc.sbuf_tensor("md_wx", [128, NDC * 160], F32))
        wxh = ps.enter_context(nc.sbuf_tensor("md_wxh", [128, NDC * 160], BF16))
        wxl = ps.enter_context(nc.sbuf_tensor("md_wxl", [128, NDC * 160], BF16))
        wdb = ps.enter_context(nc.sbuf_tensor("md_wd", [128, D], F32))
        wdh = ps.enter_context(nc.sbuf_tensor("md_wdh", [128, D], BF16))
        wdl = ps.enter_context(nc.sbuf_tensor("md_wdl", [128, D], BF16))
        dbh = ps.enter_context(nc.sbuf_tensor("md_dbh", [128, T], BF16))
        dbl = ps.enter_context(nc.sbuf_tensor("md_dbl", [128, T], BF16))
        sp = [ps.enter_context(nc.sbuf_tensor(f"md_sp{i}", [128, 512], F32)) for i in range(6)]
        dlt = [ps.enter_context(nc.sbuf_tensor(f"md_dl{i}", [128, 512], F32)) for i in range(4)]
        pdt = ps.enter_context(nc.psum_tensor("md_p0", [128, 512], F32))
        pbc = ps.enter_context(nc.psum_tensor("md_p1", [32, 512], F32))
        pd2 = [ps.enter_context(nc.psum_tensor(f"md_p2{i}", [128, 512], F32)) for i in range(2)]
        with nc.Block("md") as block:
            @block.sync
            def _(sync):
                sync.wait_ge(c.sem("mc" + "st"), 16 * 4 * NDC * 4)
                sync.dma_start(uh[:], chunked_src(scr["uh"], 0, DIN, 0, T, T)
                               ).then_inc(c.sem(P + "x"), 16)
                sync.dma_start(ul[:], chunked_src(scr["ul"], 0, DIN, 0, T, T)
                               ).then_inc(c.sem(P + "x"), 16)
                sync.dma_start(wxb[:], chunked_src(io["wx"], 0, DIN, 0, 160, 160)
                               ).then_inc(c.sem(P + "x"), 16)
                sync.dma_start(wdb[:], rows_src(io["wdt"], 0, 128, D, D)
                               ).then_inc(c.sem(P + "x"), 16)

            @block.vector
            def _(vector):
                vector.wait_ge(c.sem(P + "x"), 16 * 4)
                vector.tensor_copy(wxh[:], wxb[:])
                vector.tensor_sub(wxl[:], wxb[:], wxh[:])
                vector.tensor_copy(wdh[:], wdb[:])
                vector.tensor_sub(wdl[:], wdb[:], wdh[:]).then_inc(c.sem(P + "q"), 1)
                for tg in range(4):
                    vector.wait_ge(c.sem(P + "mm"), tg + 1)
                    vector.tensor_copy(dbh[:, tg * 512:(tg + 1) * 512], pdt[:])
                    vector.tensor_sub(dbl[:, tg * 512:(tg + 1) * 512], pdt[:],
                                      dbh[:, tg * 512:(tg + 1) * 512])
                    vector.tensor_copy(bc_sb[:, tg * 512:(tg + 1) * 512], pbc[:]
                                       ).then_inc(c.sem(P + "ev"), 1)
                for do in range(NDC):
                    for tg in range(4):
                        k = do * 4 + tg
                        vector.wait_ge(c.sem(P + "sp"), 4 * (k + 1))
                        if k >= 4:
                            vector.wait_ge(c.sem(P + "dst"), 16 * (k - 3))
                        vector.tensor_add(dlt[k % 4][:], sp[(k % 2) * 3 + 1][:],
                                          sp[(k % 2) * 3 + 2][:]).then_inc(c.sem(P + "dv"), 1)

            @block.tensor
            def _(tensor):
                tensor.wait_ge(c.sem(P + "q"), 1)
                for tg in range(4):
                    if tg >= 1:
                        tensor.wait_ge(c.sem(P + "ev"), tg)
                    first = True
                    for dc in range(NDC):
                        for (lt, rt) in ((wxh, uh), (wxh, ul), (wxl, uh)):
                            tensor.matmul(pdt[:],
                                          lt[:, dc * 160: dc * 160 + 128],
                                          rt[:, dc * T + tg * 512: dc * T + tg * 512 + 512],
                                          start=first, stop=(dc == NDC - 1 and lt is wxl))
                            first = False
                    first = True
                    for dc in range(NDC):
                        for (lt, rt) in ((wxh, uh), (wxh, ul), (wxl, uh)):
                            ins = tensor.matmul(
                                pbc[:],
                                lt[:, dc * 160 + 128: dc * 160 + 160],
                                rt[:, dc * T + tg * 512: dc * T + tg * 512 + 512],
                                start=first, stop=(dc == NDC - 1 and lt is wxl))
                            first = False
                    ins.then_inc(c.sem(P + "mm"), 1)
                tensor.wait_ge(c.sem(P + "ev"), 4)
                for do in range(NDC):
                    for tg in range(4):
                        k = do * 4 + tg
                        if k >= 2:
                            tensor.wait_ge(c.sem(P + "sp"), 4 * (k - 2) + 2)
                        first = True
                        for (lt, rt) in ((wdh, dbh), (wdh, dbl), (wdl, dbh)):
                            ins = tensor.matmul(
                                pd2[k % 2][:],
                                lt[:, do * 128:(do + 1) * 128],
                                rt[:, tg * 512:(tg + 1) * 512],
                                start=first, stop=(lt is wdl))
                            first = False
                        ins.then_inc(c.sem(P + "m2"), 1)

            @block.scalar
            def _(scalar):
                for do in range(NDC):
                    for tg in range(4):
                        k = do * 4 + tg
                        scalar.wait_ge(c.sem(P + "m2"), k + 1)
                        if k >= 4:
                            scalar.wait_ge(c.sem(P + "dst"), 16 * (k - 3))
                        b = (k % 2) * 3
                        scalar.activation(sp[b + 0][:], pd2[k % 2][:], AF.Abs,
                                          bias=g["bdt"][:, do:do + 1]).then_inc(c.sem(P + "sp"), 1)
                        scalar.activation(sp[b + 1][:], pd2[k % 2][:], AF.Relu,
                                          bias=g["bdt"][:, do:do + 1]).then_inc(c.sem(P + "sp"), 1)
                        scalar.activation(sp[b + 2][:], sp[b + 0][:], AF.Exp, scale=-1.0
                                          ).then_inc(c.sem(P + "sp"), 1)
                        scalar.activation(sp[b + 2][:], sp[b + 2][:], AF.Ln, bias=1.0
                                          ).then_inc(c.sem(P + "sp"), 1)

            @block.gpsimd
            def _(gp):
                gp.wait_ge(c.sem(P + "ev"), 4)
                gp.dma_start(rows_src(scr["bc"], 0, 32, T, T), bc_sb[:]
                             ).then_inc(c.sem(P + "dst"), 16)
                for do in range(NDC):
                    for tg in range(4):
                        k = do * 4 + tg
                        gp.wait_ge(c.sem(P + "dv"), k + 1)
                        gp.dma_start(rows_src(scr["delta"], do * 128, 128, 512, T, tg * 512),
                                     dlt[k % 4][:]).then_inc(c.sem(P + "dst"), 16)


def _emit_mamba_d2(c, g, io, scr):
    nc = c.nc
    P = "md"
    with contextlib.ExitStack() as ps:
        wob = [ps.enter_context(nc.sbuf_tensor(f"m2_wo{i}", [128, NDC * 512], F32)) for i in range(2)]
        woh = [ps.enter_context(nc.sbuf_tensor(f"m2_wh{i}", [128, NDC * 512], BF16)) for i in range(2)]
        wol = [ps.enter_context(nc.sbuf_tensor(f"m2_wl{i}", [128, NDC * 512], BF16)) for i in range(2)]
        with nc.Block("md2") as block:
            @block.sync
            def _(sync):
                for dg in range(4):
                    sync.wait_ge(c.sem(P + "q"), dg + 1)
                    sync.dma_start(wob[dg % 2][:],
                                   chunked_src(io["wout"], 0, D, dg * 512, 512, D)
                                   ).then_inc(c.sem(P + "x"), 16)

            @block.vector
            def _(vector):
                for dg in range(4):
                    vector.wait_ge(c.sem(P + "x"), 16 * (4 + dg + 1))
                    if dg >= 2:
                        vector.wait_ge(c.sem(P + "wst"), 32 * (dg - 1))
                    vector.tensor_copy(woh[dg % 2][:], wob[dg % 2][:])
                    vector.tensor_sub(wol[dg % 2][:], wob[dg % 2][:], woh[dg % 2][:]
                                      ).then_inc(c.sem(P + "q"), 1)

            @block.gpsimd
            def _(gp):
                for dg in range(4):
                    gp.wait_ge(c.sem(P + "q"), dg + 2)
                    gp.dma_start(chunked_src(scr["wouth"], 0, D, dg * 512, 512, D),
                                 woh[dg % 2][:]).then_inc(c.sem(P + "wst"), 16)
                    gp.dma_start(chunked_src(scr["woutl"], 0, D, dg * 512, 512, D),
                                 wol[dg % 2][:]).then_inc(c.sem(P + "wst"), 16)


def _emit_mamba_e(c, g, io, scr, x_src, x_dst, TCH, NCH, NTC2):
    nc = c.nc
    P = "me"
    NW = NDC * TCH
    with contextlib.ExitStack() as ps:
        deltab = [ps.enter_context(nc.sbuf_tensor(f"me_dl{i}", [128, NW], F32)) for i in range(2)]
        ub = [ps.enter_context(nc.sbuf_tensor(f"me_ub{i}", [128, NW], F32)) for i in range(2)]
        srb = ps.enter_context(nc.sbuf_tensor("me_sr", [128, NW], F32))
        xsb = ps.enter_context(nc.sbuf_tensor("me_xs", [128, NW], F32))
        dub = ps.enter_context(nc.sbuf_tensor("me_du", [128, NW], F32))
        yb = ps.enter_context(nc.sbuf_tensor("me_y", [128, NW], F32))
        ytmp = ps.enter_context(nc.sbuf_tensor("me_yt", [128, TCH], F32))
        ysr = dub
        yh = ps.enter_context(nc.sbuf_tensor("me_yh", [128, NW], BF16))
        yl = ps.enter_context(nc.sbuf_tensor("me_yl", [128, NW], BF16))
        dab = [ps.enter_context(nc.sbuf_tensor(f"me_da{i}", [128, TCH], F32)) for i in range(4)]
        dbub = [ps.enter_context(nc.sbuf_tensor(f"me_db{i}", [128, TCH], F32)) for i in range(4)]
        hb = [ps.enter_context(nc.sbuf_tensor(f"me_h{i}", [128, TCH], F32)) for i in range(4)]
        brepb = [ps.enter_context(nc.sbuf_tensor(f"me_br{i}", [128, TCH], F32)) for i in range(2)]
        crepb = [ps.enter_context(nc.sbuf_tensor(f"me_cr{i}", [128, TCH], F32)) for i in range(2)]
        wohb = ps.enter_context(nc.sbuf_tensor("me_woh", [128, NDC * 512], BF16))
        wolb = ps.enter_context(nc.sbuf_tensor("me_wol", [128, NDC * 512], BF16))
        ob = [ps.enter_context(nc.sbuf_tensor(f"me_o{i}", [128, 512], F32)) for i in range(4)]
        pw = [ps.enter_context(nc.psum_tensor(f"me_pw{i}", [128, 512], F32)) for i in range(2)]

        NT = NST * NDC

        def ld_delta_u(sync, ch):
            lp = c.sem(P + "ld%d" % (ch % 2))
            if ch >= 2:
                sync.wait_ge(c.sem(P + "da"), (ch - 1) * NT)
                sync.wait_ge(c.sem(P + "du"), (ch - 1) * NDC)
                sync.wait_ge(c.sem(P + "ost"), 16 * 8 * (ch - 1))
            sync.dma_start(deltab[ch % 2][:],
                           chunked_src(scr["delta"], 0, DIN, ch * TCH, TCH, T)
                           ).then_inc(lp, 16)
            if ch >= 2:
                sync.wait_ge(c.sem(P + "fin"), ch - 1)
            sync.dma_start(ub[ch % 2][:], chunked_src(scr["u"], 0, DIN, ch * TCH, TCH, T)
                           ).then_inc(lp, 16)

        def ld_wl(sync, ch, dg):
            j = ch * 4 + dg
            if j >= 1:
                sync.wait_ge(c.sem(P + "wmm"), j * NTC2)
            sync.dma_start(wohb[:], chunked_src(scr["wouth"], 0, D, dg * 512, 512, D)
                           ).then_inc(c.sem(P + "wl"), 16)
            sync.dma_start(wolb[:], chunked_src(scr["woutl"], 0, D, dg * 512, 512, D)
                           ).then_inc(c.sem(P + "wl"), 16)

        with nc.Block("me") as block:
            @block.sync
            def _(sync):
                sync.wait_ge(c.sem("md" + "dst"), 16 * (NDC * 4 + 1))
                sync.wait_ge(c.sem("md" + "wst"), 16 * 8)
                sync.wait_ge(c.sem("mc" + "st"), 16 * 4 * NDC * 4)
                ld_delta_u(sync, 0)
                for ch in range(NCH):
                    if ch >= 1:
                        sync.wait_ge(c.sem(P + "wmm"), ch * 4 * NTC2)
                    sync.dma_start(srb[:], chunked_src(scr["sr"], 0, DIN, ch * TCH, TCH, T)
                                   ).then_inc(c.sem(P + "ld"), 16)
                    sync.dma_start(xsb[:], chunked_src(scr["xs"], 0, DIN, ch * TCH, TCH, T)
                                   ).then_inc(c.sem(P + "ld"), 16)
                    ld_wl(sync, ch, 0)
                    if ch + 1 < NCH:
                        ld_delta_u(sync, ch + 1)
                    sync.wait_ge(c.sem(P + "da"), (ch + 1) * NT)
                    sync.wait_ge(c.sem(P + "du"), (ch + 1) * NDC)
                    sync.dma_start(deltab[ch % 2][:],
                                   chunked_src(x_src, ch * TCH, TCH, 0, D, D)
                                   ).then_inc(c.sem(P + "sel"), 16)
                    for dg in range(1, 4):
                        ld_wl(sync, ch, dg)

            def evac(scalar, ch2, e0):
                e = ch2 * 4 * NTC2 + e0
                scalar.wait_ge(c.sem(P + "wmm"), e + 1)
                if e >= 4:
                    scalar.wait_ge(c.sem(P + "ost"), 16 * (e - 3))
                scalar.activation(ob[e % 4][:], pw[e0 % 2][:], AF.Identity
                                  ).then_inc(c.sem(P + "ev"), 1)

            @block.scalar
            def _(scalar):
                for ch in range(NCH):
                    scalar.wait_ge(c.sem(P + "ld%d" % (ch % 2)), 16 * 2 * (ch // 2 + 1))
                    for n in range(NST):
                        for dc in range(NDC):
                            t = n * NDC + dc
                            if ch >= 1 and t % 16 == 0:
                                if 1 <= t // 16 <= 4:
                                    evac(scalar, ch - 1, t // 16 - 1)
                                elif t // 16 >= 12:
                                    evac(scalar, ch - 1, t // 16 - 8)
                            idx = ch * NT + t
                            if idx >= 4:
                                scalar.wait_ge(c.sem(P + "sc"), idx - 3)
                            scalar.activation(dab[idx % 4][:],
                                              deltab[ch % 2][:, dc * TCH:(dc + 1) * TCH],
                                              AF.Exp,
                                              scale=g["acol"][:, dc * NST + n: dc * NST + n + 1]
                                              ).then_inc(c.sem(P + "da"), 1)
                            if idx >= 4 and idx - 4 < (NCH - 1) * NT:
                                pn, pdc = divmod((idx - 4) % NT, NDC)
                                scalar.activation(
                                    g["hstate"][:, pdc * NST + pn: pdc * NST + pn + 1],
                                    hb[(idx - 4) % 4][:, TCH - 1: TCH], AF.Identity)
                for e0 in range(4 * NTC2):
                    evac(scalar, NCH - 1, e0)

            def bc_pair(gp, ch2, n2):
                gp.dma_start(brepb[n2 % 2][:],
                             bass.AP(scr["bc"].tensor, n2 * T + ch2 * TCH,
                                     [[0, 128], [1, TCH]])
                             ).then_inc(c.sem(P + "bcl"), 16)
                gp.dma_start(crepb[n2 % 2][:],
                             bass.AP(scr["bc"].tensor, (16 + n2) * T + ch2 * TCH,
                                     [[0, 128], [1, TCH]])
                             ).then_inc(c.sem(P + "bcl"), 16)

            @block.gpsimd
            def _(gp):
                for ch in range(NCH):
                    for n in range(NST):
                        k = ch * NST + n
                        if k == 0:
                            gp.wait_ge(c.sem("md" + "dst"), 16 * (NDC * 4 + 1))
                            bc_pair(gp, 0, 0)
                        if k + 1 < NCH * NST:
                            gp.wait_ge(c.sem(P + "sc"), k * NDC + 12)
                            bc_pair(gp, (k + 1) // NST, (k + 1) % NST)
                        if ch >= 1 and n in (2, 4, 6, 8):
                            _me_add_stores(c, gp, ch - 1, n // 2 - 1, n // 2, ob,
                                           deltab[(ch - 1) % 2], x_dst, TCH, NTC2)
                    if ch >= 1:
                        _me_add_stores(c, gp, ch - 1, 4, 4 * NTC2, ob,
                                       deltab[(ch - 1) % 2], x_dst, TCH, NTC2)
                _me_add_stores(c, gp, NCH - 1, 0, 4 * NTC2, ob,
                               deltab[(NCH - 1) % 2], x_dst, TCH, NTC2)

            @block.vector
            def _(vector):
                for ch in range(NCH):
                    vector.wait_ge(c.sem(P + "ld%d" % (ch % 2)), 16 * 2 * (ch // 2 + 1))
                    for dc in range(NDC):
                        vector.tensor_mul(dub[:, dc * TCH:(dc + 1) * TCH],
                                          deltab[ch % 2][:, dc * TCH:(dc + 1) * TCH],
                                          ub[ch % 2][:, dc * TCH:(dc + 1) * TCH]
                                          ).then_inc(c.sem(P + "du"), 1)
                    for n in range(NST):
                        vector.wait_ge(c.sem(P + "bcl"), 32 * (ch * NST + n + 1))
                        for dc in range(NDC):
                            idx = ch * NT + n * NDC + dc
                            vector.wait_ge(c.sem(P + "da"), idx + 1)
                            vector.tensor_mul(dbub[idx % 4][:],
                                              dub[:, dc * TCH:(dc + 1) * TCH],
                                              brepb[n % 2][:])
                            vector.tensor_tensor_scan(
                                hb[idx % 4][:], dab[idx % 4][:], dbub[idx % 4][:],
                                g["hstate"][:, dc * NST + n: dc * NST + n + 1],
                                ALU.mult, ALU.add).then_inc(c.sem(P + "sc"), 1)
                            if n == 0:
                                vector.tensor_mul(yb[:, dc * TCH:(dc + 1) * TCH],
                                                  hb[idx % 4][:], crepb[n % 2][:])
                            else:
                                vector.tensor_mul(ytmp[:], hb[idx % 4][:], crepb[n % 2][:])
                                vector.tensor_add(yb[:, dc * TCH:(dc + 1) * TCH],
                                                  yb[:, dc * TCH:(dc + 1) * TCH], ytmp[:])
                    vector.wait_ge(c.sem(P + "ld"), 16 * 2 * (ch + 1))
                    if ch >= 1:
                        vector.wait_ge(c.sem(P + "wmm"), ch * 4 * NTC2)
                    for dc in range(NDC):
                        vector.scalar_tensor_tensor(
                            yb[:, dc * TCH:(dc + 1) * TCH], ub[ch % 2][:, dc * TCH:(dc + 1) * TCH],
                            g["dvec"][:, dc:dc + 1], yb[:, dc * TCH:(dc + 1) * TCH],
                            ALU.mult, ALU.add)
                        vector.tensor_mul(ysr[:, dc * TCH:(dc + 1) * TCH],
                                          yb[:, dc * TCH:(dc + 1) * TCH],
                                          srb[:, dc * TCH:(dc + 1) * TCH])
                        vector.tensor_copy(yh[:, dc * TCH:(dc + 1) * TCH],
                                           ysr[:, dc * TCH:(dc + 1) * TCH])
                        vector.tensor_sub(yl[:, dc * TCH:(dc + 1) * TCH],
                                          ysr[:, dc * TCH:(dc + 1) * TCH],
                                          yh[:, dc * TCH:(dc + 1) * TCH])
                    vector.nop().then_inc(c.sem(P + "fin"), 1)

            @block.tensor
            def _(tensor):
                for ch in range(NCH):
                    _me_wout_mms(c, tensor, ch, yh, yl, wohb, wolb, xsb,
                                 g, pw, TCH, NTC2)


def _me_wout_mms(c, tensor, ch, yh, yl, wohb, wolb, xsb, g, pw, TCH, NTC2):
    P = "me"
    tensor.wait_ge(c.sem(P + "fin"), ch + 1)
    tensor.wait_ge(c.sem(P + "ld"), 16 * 2 * (ch + 1))
    for dg in range(4):
        j = ch * 4 + dg
        tensor.wait_ge(c.sem(P + "wl"), 32 * (j + 1))
        for i in range(NTC2):
            e = ch * 4 * NTC2 + dg * NTC2 + i
            if e >= 2:
                tensor.wait_ge(c.sem(P + "ev"), e - 1)
            first = True
            for dc in range(NDC):
                for (lt, rt) in ((yh, wohb), (yh, wolb), (yl, wohb)):
                    tensor.matmul(
                        pw[i % 2][:],
                        lt[:, dc * TCH + i * 128: dc * TCH + (i + 1) * 128],
                        rt[:, dc * 512:(dc + 1) * 512],
                        start=first, stop=False)
                    first = False
            for jj in range(4):
                oc = dg * 4 + jj
                ins = tensor.matmul(
                    pw[i % 2][:, jj * 128:(jj + 1) * 128],
                    xsb[:, oc * TCH + i * 128: oc * TCH + (i + 1) * 128],
                    g["eye"][:],
                    start=False, stop=(jj == 3))
            ins.then_inc(c.sem(P + "wmm"), 1)


def _me_add_stores(c, gp, ch, e0lo, e0hi, ob, x1b, x_dst, TCH, NTC2):
    P = "me"
    if e0lo == 0:
        gp.wait_ge(c.sem(P + "sel"), 16 * (ch + 1))
    for e0 in range(e0lo, e0hi):
        dg, i = divmod(e0, NTC2)
        e = ch * 4 * NTC2 + e0
        gp.wait_ge(c.sem(P + "ev"), e + 1)
        gp.tensor_add(ob[e % 4][:], ob[e % 4][:],
                      x1b[:, i * D + dg * 512: i * D + dg * 512 + 512])
        gp.dma_start(rows_src(x_dst, ch * TCH + i * 128, 128, 512, D, dg * 512),
                     ob[e % 4][:]).then_inc(c.sem(P + "ost"), 16)


def build_program(dbg=()):
    nc = bass.Bass("TRN2", target_bir_lowering=False, detect_race_conditions=False)

    io = {}
    io["x_in"] = nc.dram_tensor("x_in", [T, D], F32, kind="ExternalInput").ap()
    io["w1t"] = nc.dram_tensor("w1t", [D, DFF], F32, kind="ExternalInput").ap()
    io["w2t"] = nc.dram_tensor("w2t", [DFF, D], F32, kind="ExternalInput").ap()
    io["w_in"] = nc.dram_tensor("w_in", [D, 2 * DIN], F32, kind="ExternalInput").ap()
    io["convw"] = nc.dram_tensor("convw", [KCV * DIN, DIN], F32, kind="ExternalInput").ap()
    io["wx"] = nc.dram_tensor("wx", [DIN, DTR + 2 * NST], F32, kind="ExternalInput").ap()
    io["wdt"] = nc.dram_tensor("wdt", [DTR, DIN], F32, kind="ExternalInput").ap()
    io["bdt"] = nc.dram_tensor("bdt", [DIN, 1], F32, kind="ExternalInput").ap()
    io["convb"] = nc.dram_tensor("convb", [DIN, 1], F32, kind="ExternalInput").ap()
    io["alog"] = nc.dram_tensor("alog", [DIN, NST], F32, kind="ExternalInput").ap()
    io["dvec"] = nc.dram_tensor("dvec", [DIN, 1], F32, kind="ExternalInput").ap()
    io["wout"] = nc.dram_tensor("wout", [D, D], F32, kind="ExternalInput").ap()
    io["eye"] = nc.dram_tensor("eye", [128, 128], F32, kind="ExternalInput").ap()
    io["ones1"] = nc.dram_tensor("ones1", [1, 128], F32, kind="ExternalInput").ap()
    io["y_out"] = nc.dram_tensor("y_out", [T, D], F32, kind="ExternalOutput").ap()

    def dscr(name, shape, dtype):
        kind = "ExternalOutput" if name in dbg else "Internal"
        return nc.dram_tensor(name, shape, dtype, kind=kind).ap()

    scr = {}
    scr["x1"] = dscr("scr_x1", [T, D], F32)
    scr["x2"] = dscr("scr_x2", [T, D], F32)
    scr["iq"] = dscr("scr_iq", [T, D], BF16)
    scr["h"] = dscr("scr_h", [T, DFF], F32)
    scr["ih"] = dscr("scr_ih", [T, DFF], BF16)
    scr["iw1"] = dscr("scr_iw1", [D, DFF], BF16)
    scr["iw2"] = dscr("scr_iw2", [DFF, D], BF16)
    scr["xch"] = dscr("scr_xch", [DIN, T], BF16)
    scr["xcl"] = dscr("scr_xcl", [DIN, T], BF16)
    scr["sr"] = dscr("scr_sr", [DIN, T], F32)
    scr["xs"] = dscr("scr_xs", [DIN, T], F32)
    scr["u"] = dscr("scr_u", [DIN, T], F32)
    scr["uh"] = dscr("scr_uh", [DIN, T], BF16)
    scr["ul"] = dscr("scr_ul", [DIN, T], BF16)
    scr["delta"] = dscr("scr_delta", [DIN, T], F32)
    scr["bc"] = dscr("scr_bc", [32, T], F32)
    scr["wouth"] = dscr("scr_wouth", [D, D], BF16)
    scr["woutl"] = dscr("scr_woutl", [D, D], BF16)

    es = contextlib.ExitStack()
    with es:
        c = Ctx(nc, es)
        g = {}
        g["eye"] = es.enter_context(nc.sbuf_tensor("g_eye", [128, 128], F32))
        g["ones1"] = es.enter_context(nc.sbuf_tensor("g_ones1", [1, 128], F32))
        g["c1536"] = es.enter_context(nc.sbuf_tensor("g_c1536", [128, 1], F32))
        g["ceps"] = es.enter_context(nc.sbuf_tensor("g_ceps", [128, 1], F32))
        g["sw1"] = es.enter_context(nc.sbuf_tensor("g_sw1", [128, 1], F32))
        g["rw1"] = es.enter_context(nc.sbuf_tensor("g_rw1", [128, 1], F32))
        g["sw2"] = es.enter_context(nc.sbuf_tensor("g_sw2", [128, 1], F32))
        g["rw2"] = es.enter_context(nc.sbuf_tensor("g_rw2", [128, 1], F32))
        g["dq1"] = es.enter_context(nc.sbuf_tensor("g_dq1", [128, NTC], F32))
        g["dq2h"] = es.enter_context(nc.sbuf_tensor("g_dq2h", [128, NTC], F32))
        g["hmax"] = es.enter_context(nc.sbuf_tensor("g_hmax", [128, NTC * NFG], F32))
        g["acol"] = es.enter_context(nc.sbuf_tensor("g_acol", [128, NDC * NST], F32))
        g["bdt"] = es.enter_context(nc.sbuf_tensor("g_bdt", [128, NDC], F32))
        g["convb"] = es.enter_context(nc.sbuf_tensor("g_convb", [128, NDC], F32))
        g["dvec"] = es.enter_context(nc.sbuf_tensor("g_dvec", [128, NDC], F32))
        g["hstate"] = es.enter_context(nc.sbuf_tensor("g_hstate", [128, NDC * NST], F32))

        _emit_p0(c, io, g, scr)
        _emit_ffn(c, 1, dict(io, x_src=io["x_in"],
                             x_dst=(io["y_out"] if BUILD_PHASES == 1 else scr["x1"])),
                  g, scr)
        if BUILD_PHASES >= 2:
            _emit_mamba(c, dict(io, x_src=scr["x1"],
                                x_dst=(io["y_out"] if BUILD_PHASES == 2 else scr["x2"])),
                        g, scr)
        if BUILD_PHASES >= 3:
            _emit_ffn(c, 2, dict(io, x_src=scr["x2"], x_dst=io["y_out"]), g, scr)

        with nc.Block("fin") as block:
            @block.sync
            def _(sync):
                for s, n in list(c.cnt.items()):
                    if s.endswith("ost") and n:
                        sync.wait_ge(c.sems[s], n)
    return nc


def kernel(L0_attn_x, L0_attn_W_in, L0_attn_conv_w, L0_attn_conv_b,
           L0_attn_W_x, L0_attn_W_dt, L0_attn_b_dt, L0_attn_A_log,
           L0_attn_D, L0_attn_W_out, L0_ffn_W1, L0_ffn_W2):
    x = np.ascontiguousarray(L0_attn_x, dtype=np.float32)
    shared = dict(
        w1t=np.ascontiguousarray(np.asarray(L0_ffn_W1, np.float32).T),
        w2t=np.ascontiguousarray(np.asarray(L0_ffn_W2, np.float32).T),
        w_in=np.ascontiguousarray(L0_attn_W_in, np.float32),
        convw=np.ascontiguousarray(
            np.asarray(L0_attn_conv_w, np.float32).transpose(2, 1, 0).reshape(KCV * DIN, DIN)),
        wx=np.ascontiguousarray(L0_attn_W_x, np.float32),
        wdt=np.ascontiguousarray(L0_attn_W_dt, np.float32),
        bdt=np.ascontiguousarray(L0_attn_b_dt, np.float32).reshape(DIN, 1),
        convb=np.ascontiguousarray(L0_attn_conv_b, np.float32).reshape(DIN, 1),
        alog=np.ascontiguousarray(L0_attn_A_log, np.float32),
        dvec=np.ascontiguousarray(L0_attn_D, np.float32).reshape(DIN, 1),
        wout=np.ascontiguousarray(L0_attn_W_out, np.float32),
        eye=np.eye(128, dtype=np.float32),
        ones1=np.ones((1, 128), np.float32),
    )
    nc = build_program()
    in_maps = [dict(shared, x_in=np.ascontiguousarray(x[b])) for b in range(8)]
    res = run_bass_kernel_spmd(nc, in_maps, core_ids=list(range(8)))
    return np.stack([r["y_out"] for r in res.results], axis=0)


# version 20
# speedup vs baseline: 1.0322x; 1.0322x over previous
import contextlib
import numpy as np
import concourse.bass as bass
import concourse.mybir as mybir
from concourse.bass_utils import run_bass_kernel_spmd

F32 = mybir.dt.float32
F16 = mybir.dt.float16
BF16 = mybir.dt.bfloat16
AF = mybir.ActivationFunctionType
ALU = mybir.AluOpType
AX = mybir.AxisListType.X

T = 2048
D = 2048
DFF = 8192
DIN = 2048
NST = 16
DTR = 128
KCV = 4
NTC = T // 128
NDC = D // 128
NFC = DFF // 128
NFG = DFF // 512

BUILD_PHASES = 3


class Ctx:
    def __init__(self, nc, es):
        self.nc = nc
        self.es = es
        self.sems = {}
        self.cnt = {}

    def sem(self, name):
        if name not in self.sems:
            self.sems[name] = self.es.enter_context(self.nc.semaphore(name))
            self.cnt[name] = 0
        return self.sems[name]

    def guard(self, eng, name, ins):
        ins.then_inc(self.sem(name), 1)
        self.cnt[name] += 1
        eng.wait_ge(self.sems[name], self.cnt[name])


def chunked_src(dram_ap, row0, nrows, col0, ncols, row_stride):
    nchunk = nrows // 128
    off = row0 * row_stride + col0
    return bass.AP(dram_ap.tensor, off,
                   [[row_stride, 128], [128 * row_stride, nchunk], [1, ncols]])


def rows_src(dram_ap, row0, nrows, ncols, row_stride, col0=0):
    return bass.AP(dram_ap.tensor, row0 * row_stride + col0,
                   [[row_stride, nrows], [1, ncols]])


def _emit_p0(c, io, g, scr):
    nc = c.nc
    NT1 = NDC
    NT2 = NFC // 4
    NTILES = NT1 + NT2
    nconst = 2 + 4 * NDC
    P = "f1"
    x_src = io["x_in"]
    with contextlib.ExitStack() as ps:
        buf = [ps.enter_context(nc.sbuf_tensor(f"p0_buf{i}", [128, DFF], F32)) for i in range(2)]
        dump = ps.enter_context(nc.sbuf_tensor("p0_dump", [128, DFF], F32))
        acc = ps.enter_context(nc.sbuf_tensor("p0_acc", [128, NTILES], F32))
        col = ps.enter_context(nc.sbuf_tensor("p0_col", [128, 8], F32))
        onec = ps.enter_context(nc.sbuf_tensor("p0_onec", [128, 1], F32))
        pscl = ps.enter_context(nc.psum_tensor("p0_ps", [128, 16], F32))
        xa = [ps.enter_context(nc.sbuf_tensor(f"{P}a_x{i}", [128, D], F32)) for i in range(2)]
        xln = [ps.enter_context(nc.sbuf_tensor(f"{P}a_ln{i}", [128, D], F32)) for i in range(2)]
        sqd = ps.enter_context(nc.sbuf_tensor(f"{P}a_sqd", [128, D], F32))
        q16 = ps.enter_context(nc.sbuf_tensor(f"{P}a_q16", [128, D], F16))
        iqb = [ps.enter_context(nc.sbuf_tensor(f"{P}a_iq{i}", [128, D], BF16)) for i in range(2)]
        cl = ps.enter_context(nc.sbuf_tensor(f"{P}a_col", [128, 8], F32))

        with nc.Block("p0") as block:
            @block.sync
            def _(sync):
                sync.dma_start(g["eye"][:], io["eye"]).then_inc(c.sem("p0ld"), 16)
                sync.dma_start(g["ones1"][:], io["ones1"]).then_inc(c.sem("p0ld"), 16)
                for dc in range(NDC):
                    sync.dma_start(g["acol"][:, dc * NST:(dc + 1) * NST],
                                   rows_src(io["alog"], dc * 128, 128, NST, NST)
                                   ).then_inc(c.sem("p0ld"), 16)
                    sync.dma_start(g["bdt"][:, dc:dc + 1],
                                   rows_src(io["bdt"], dc * 128, 128, 1, 1)
                                   ).then_inc(c.sem("p0ld"), 16)
                    sync.dma_start(g["convb"][:, dc:dc + 1],
                                   rows_src(io["convb"], dc * 128, 128, 1, 1)
                                   ).then_inc(c.sem("p0ld"), 16)
                    sync.dma_start(g["dvec"][:, dc:dc + 1],
                                   rows_src(io["dvec"], dc * 128, 128, 1, 1)
                                   ).then_inc(c.sem("p0ld"), 16)
                for i in range(NTILES):
                    if i >= 2:
                        sync.wait_ge(c.sem("p0f"), i)
                    if i < NT1:
                        srcs = [chunked_src(io["w1t"], i * 128, 128, h * (DFF // 2),
                                            DFF // 2, DFF) for h in range(2)]
                    else:
                        srcs = [chunked_src(io["w2t"], (i - NT1) * 512 + h * 256, 256,
                                            0, D, D) for h in range(2)]
                    for h in range(2):
                        sync.dma_start(buf[i % 2][:, h * (DFF // 2):(h + 1) * (DFF // 2)],
                                       srcs[h]).then_inc(c.sem("p0w"), 16)
                    tc = i - 2
                    if 0 <= tc < NTC:
                        if tc >= 2:
                            sync.wait_ge(c.sem(P + "av"), 3 * (tc - 2) + 3)
                        sync.dma_start(xa[tc % 2][:], rows_src(x_src, tc * 128, 128, D, D)
                                       ).then_inc(c.sem(P + "ax" + str(tc % 2)), 16)

            @block.gpsimd
            def _(gp):
                for tc in range(NTC):
                    gp.wait_ge(c.sem(P + "av"), 3 * tc + 3)
                    gp.dma_start(rows_src(scr["iq"], tc * 128, 128, D, D), iqb[tc % 2][:]
                                 ).then_inc(c.sem(P + "ast"), 16)

            @block.vector
            def _(vector):
                vector.memset(g["c1536"][:], 1536.0)
                vector.memset(g["ceps"][:], 1e-5)
                vector.memset(g["hstate"][:], 0.0)
                vector.memset(onec[:], 1.0)

            @block.scalar
            def _(scalar):
                def trio(tc):
                    scalar.wait_ge(c.sem(P + "ax" + str(tc % 2)), 16 * (tc // 2 + 1))
                    scalar.activation(sqd[:], xa[tc % 2][:], AF.Square
                                      ).then_inc(c.sem(P + "as"), 1)
                    scalar.wait_ge(c.sem(P + "av"), 3 * tc + 1)
                    scalar.activation(cl[:, 4:5], cl[:, 3:4], AF.Sqrt,
                                      bias=g["ceps"][:, 0:1]).then_inc(c.sem(P + "as"), 1)
                    scalar.wait_ge(c.sem(P + "av"), 3 * tc + 2)
                    scalar.activation(xln[tc % 2][:], xa[tc % 2][:], AF.Identity,
                                      scale=cl[:, 4:5], bias=cl[:, 5:6]
                                      ).then_inc(c.sem(P + "as"), 1)

                scalar.wait_ge(c.sem("p0ld"), 16 * nconst)
                scalar.activation(g["acol"][:], g["acol"][:], AF.Exp
                                  ).then_inc(c.sem("p0f"), 1)
                for i in range(NTILES):
                    scalar.wait_ge(c.sem("p0w"), 32 * (i + 1))
                    scalar.activation(dump[:], buf[i % 2][:], AF.Abs,
                                      accum_out=acc[:, i:i + 1]).then_inc(c.sem("p0f"), 1)
                    if 0 <= i - 2 < NTC:
                        trio(i - 2)
                scalar.activation(dump[:, 0:128], dump[:, 0:128], AF.Identity)
                scalar.activation(dump[:, 0:128], dump[:, 0:128], AF.Identity
                                  ).then_inc(c.sem("p0f"), 1)

            @block.vector
            def _(vector):
                G = "gv"
                for tc in range(NTC):
                    vector.wait_ge(c.sem(P + "ax" + str(tc % 2)), 16 * (tc // 2 + 1))
                    c.guard(vector, G, vector.reduce_sum(cl[:, 0:1], xa[tc % 2][:], AX))
                    vector.wait_ge(c.sem(P + "as"), 3 * tc + 1)
                    c.guard(vector, G, vector.reduce_sum(cl[:, 1:2], sqd[:], AX))
                    c.guard(vector, G, vector.tensor_scalar_mul(cl[:, 2:3], cl[:, 0:1], 1.0 / D))
                    c.guard(vector, G, vector.tensor_scalar_mul(cl[:, 3:4], cl[:, 1:2], 1.0 / D))
                    c.guard(vector, G, vector.tensor_mul(cl[:, 0:1], cl[:, 2:3], cl[:, 2:3]))
                    vector.tensor_sub(cl[:, 3:4], cl[:, 3:4], cl[:, 0:1]
                                      ).then_inc(c.sem(P + "av"), 1)
                    vector.wait_ge(c.sem(P + "as"), 3 * tc + 2)
                    c.guard(vector, G, vector.reciprocal(cl[:, 4:5], cl[:, 4:5]))
                    c.guard(vector, G, vector.tensor_mul(cl[:, 5:6], cl[:, 2:3], cl[:, 4:5]))
                    vector.tensor_scalar_mul(cl[:, 5:6], cl[:, 5:6], -1.0
                                             ).then_inc(c.sem(P + "av"), 1)
                    vector.wait_ge(c.sem(P + "as"), 3 * tc + 3)
                    if tc >= 2:
                        vector.wait_ge(c.sem(P + "ast"), 16 * (tc - 1))
                    c.guard(vector, G, vector.reduce_max(cl[:, 6:7], xln[tc % 2][:], AX,
                                                         apply_absolute_value=True))
                    c.guard(vector, G, vector.tensor_scalar_max(cl[:, 6:7], cl[:, 6:7], 1e-5))
                    c.guard(vector, G, vector.tensor_scalar_mul(cl[:, 0:1], cl[:, 6:7], 1.0 / 127.0))
                    vector.tensor_copy(g["dq1"][:, tc:tc + 1], cl[:, 0:1])
                    c.guard(vector, G, vector.reciprocal(cl[:, 7:8], cl[:, 6:7]))
                    c.guard(vector, G, vector.tensor_scalar_mul(cl[:, 7:8], cl[:, 7:8], 127.0))
                    vector.tensor_scalar(q16[:], xln[tc % 2][:], cl[:, 7:8], 1536.0,
                                         ALU.mult, ALU.add)
                    vector.tensor_scalar_add(iqb[tc % 2][:], q16[:], -1536.0
                                             ).then_inc(c.sem(P + "av"), 1)

            @block.vector
            def _(vector):
                vector.wait_ge(c.sem("p0f"), 1)
                vector.tensor_scalar_mul(g["acol"][:], g["acol"][:], -1.0)
                vector.wait_ge(c.sem("p0f"), NTILES + 2)
                vector.reduce_sum(col[:, 0:1], acc[:, 0:NT1], AX)
                vector.reduce_sum(col[:, 1:2], acc[:, NT1:NTILES], AX
                                  ).then_inc(c.sem("p0m"), 1)

            @block.tensor
            def _(tensor):
                tensor.wait_ge(c.sem("p0m"), 1)
                tensor.matmul(pscl[0:1, 0:1], col[:, 0:1], onec[:], start=True, stop=True)
                tensor.matmul(pscl[0:1, 1:2], col[:, 1:2], onec[:], start=True, stop=True
                              ).then_inc(c.sem("p0m"), 1)

            @block.vector
            def _(vector):
                vector.wait_ge(c.sem("p0m"), 2)
                c.guard(vector, "gv", vector.tensor_scalar(
                    col[0:1, 2:4], pscl[0:1, 0:2],
                    1.0 / (D * DFF), 1e-5, ALU.mult, ALU.max))
                vector.reciprocal(col[0:1, 4:6], col[0:1, 2:4]).then_inc(c.sem("p0m"), 1)

            @block.tensor
            def _(tensor):
                tensor.wait_ge(c.sem("p0m"), 3)
                tensor.matmul(pscl[:, 8:12], g["ones1"][:], col[0:1, 2:6],
                              start=True, stop=True).then_inc(c.sem("p0m"), 1)

            @block.vector
            def _(vector):
                vector.wait_ge(c.sem("p0m"), 4)
                vector.tensor_copy(g["rw1"][:], pscl[:, 8:9])
                vector.tensor_copy(g["rw2"][:], pscl[:, 9:10])
                vector.tensor_copy(g["sw1"][:], pscl[:, 10:11])
                vector.tensor_copy(g["sw2"][:], pscl[:, 11:12])
                vector.tensor_scalar(g["dq1"][:, 0:NTC], g["dq1"][:, 0:NTC],
                                     g["rw1"][:, 0:1], 0.0, ALU.mult, ALU.add)


def _emit_ffn(c, call, io, g, scr):
    nc = c.nc
    x_src, x_dst = io["x_src"], io["x_dst"]
    P = f"f{call}"

    if call != 1:
        _emit_ffn_a(c, call, io, g, scr)
    _emit_ffn_rest(c, call, io, g, scr)


def _emit_ffn_a(c, call, io, g, scr):
    nc = c.nc
    x_src = io["x_src"]
    P = f"f{call}"
    with contextlib.ExitStack() as ps:
        xa = [ps.enter_context(nc.sbuf_tensor(f"{P}a_x{i}", [128, D], F32)) for i in range(2)]
        xln = [ps.enter_context(nc.sbuf_tensor(f"{P}a_ln{i}", [128, D], F32)) for i in range(2)]
        sqd = ps.enter_context(nc.sbuf_tensor(f"{P}a_sqd", [128, D], F32))
        q16 = ps.enter_context(nc.sbuf_tensor(f"{P}a_q16", [128, D], F16))
        iqb = [ps.enter_context(nc.sbuf_tensor(f"{P}a_iq{i}", [128, D], BF16)) for i in range(2)]
        cl = ps.enter_context(nc.sbuf_tensor(f"{P}a_col", [128, 8], F32))
        with nc.Block(f"{P}a") as block:
            @block.sync
            def _(sync):
                for tc in range(NTC):
                    if tc >= 2:
                        sync.wait_ge(c.sem(P + "av"), 3 * (tc - 2) + 3)
                    sync.dma_start(xa[tc % 2][:], rows_src(x_src, tc * 128, 128, D, D)
                                   ).then_inc(c.sem(P + "ax" + str(tc % 2)), 16)

            @block.gpsimd
            def _(gp):
                for tc in range(NTC):
                    gp.wait_ge(c.sem(P + "av"), 3 * tc + 3)
                    gp.dma_start(rows_src(scr["iq"], tc * 128, 128, D, D), iqb[tc % 2][:]
                                 ).then_inc(c.sem(P + "ast"), 16)

            @block.scalar
            def _(scalar):
                for tc in range(NTC):
                    scalar.wait_ge(c.sem(P + "ax" + str(tc % 2)), 16 * (tc // 2 + 1))
                    scalar.activation(sqd[:], xa[tc % 2][:], AF.Square
                                      ).then_inc(c.sem(P + "as"), 1)
                    scalar.wait_ge(c.sem(P + "av"), 3 * tc + 1)
                    scalar.activation(cl[:, 4:5], cl[:, 3:4], AF.Sqrt,
                                      bias=g["ceps"][:, 0:1]).then_inc(c.sem(P + "as"), 1)
                    scalar.wait_ge(c.sem(P + "av"), 3 * tc + 2)
                    scalar.activation(xln[tc % 2][:], xa[tc % 2][:], AF.Identity,
                                      scale=cl[:, 4:5], bias=cl[:, 5:6]
                                      ).then_inc(c.sem(P + "as"), 1)

            @block.vector
            def _(vector):
                G = "gv"
                for tc in range(NTC):
                    vector.wait_ge(c.sem(P + "ax" + str(tc % 2)), 16 * (tc // 2 + 1))
                    c.guard(vector, G, vector.reduce_sum(cl[:, 0:1], xa[tc % 2][:], AX))
                    vector.wait_ge(c.sem(P + "as"), 3 * tc + 1)
                    c.guard(vector, G, vector.reduce_sum(cl[:, 1:2], sqd[:], AX))
                    c.guard(vector, G, vector.tensor_scalar_mul(cl[:, 2:3], cl[:, 0:1], 1.0 / D))
                    c.guard(vector, G, vector.tensor_scalar_mul(cl[:, 3:4], cl[:, 1:2], 1.0 / D))
                    c.guard(vector, G, vector.tensor_mul(cl[:, 0:1], cl[:, 2:3], cl[:, 2:3]))
                    vector.tensor_sub(cl[:, 3:4], cl[:, 3:4], cl[:, 0:1]
                                      ).then_inc(c.sem(P + "av"), 1)
                    vector.wait_ge(c.sem(P + "as"), 3 * tc + 2)
                    c.guard(vector, G, vector.reciprocal(cl[:, 4:5], cl[:, 4:5]))
                    c.guard(vector, G, vector.tensor_mul(cl[:, 5:6], cl[:, 2:3], cl[:, 4:5]))
                    vector.tensor_scalar_mul(cl[:, 5:6], cl[:, 5:6], -1.0
                                             ).then_inc(c.sem(P + "av"), 1)
                    vector.wait_ge(c.sem(P + "as"), 3 * tc + 3)
                    if tc >= 2:
                        vector.wait_ge(c.sem(P + "ast"), 16 * (tc - 1))
                    c.guard(vector, G, vector.reduce_max(cl[:, 6:7], xln[tc % 2][:], AX,
                                                         apply_absolute_value=True))
                    c.guard(vector, G, vector.tensor_scalar_max(cl[:, 6:7], cl[:, 6:7], 1e-5))
                    c.guard(vector, G, vector.tensor_scalar_mul(cl[:, 0:1], cl[:, 6:7], 1.0 / 127.0))
                    vector.tensor_mul(g["dq1"][:, tc:tc + 1], cl[:, 0:1], g["rw1"][:, 0:1])
                    c.guard(vector, G, vector.reciprocal(cl[:, 7:8], cl[:, 6:7]))
                    c.guard(vector, G, vector.tensor_scalar_mul(cl[:, 7:8], cl[:, 7:8], 127.0))
                    vector.tensor_scalar(q16[:], xln[tc % 2][:], cl[:, 7:8], 1536.0,
                                         ALU.mult, ALU.add)
                    vector.tensor_scalar_add(iqb[tc % 2][:], q16[:], -1536.0
                                             ).then_inc(c.sem(P + "av"), 1)


def _emit_ffn_rest(c, call, io, g, scr):
    nc = c.nc
    x_src, x_dst = io["x_src"], io["x_dst"]
    P = f"f{call}"
    with contextlib.ExitStack() as ps:
        iqf = ps.enter_context(nc.sbuf_tensor(f"{P}c_iqf", [128, NDC * T], BF16))
        iw1 = [ps.enter_context(nc.sbuf_tensor(f"{P}c_iw1{i}", [128, NDC * 512], BF16))
               for i in range(2)]
        if call == 1:
            w1b = [ps.enter_context(nc.sbuf_tensor(f"{P}c_w1{i}", [128, NDC * 512], F32))
                   for i in range(2)]
            q16w = [ps.enter_context(nc.sbuf_tensor(f"{P}c_q16w{i}", [128, 512], F16))
                    for i in range(2)]
        hbuf = [ps.enter_context(nc.sbuf_tensor(f"{P}c_h{i}", [128, 512], F32)) for i in range(4)]
        pmm = [ps.enter_context(nc.psum_tensor(f"{P}c_ps{i}", [128, 512], F32)) for i in range(4)]

        with nc.Block(f"{P}c") as block:
            def ld_w1(sync, fg):
                if call == 1:
                    if fg >= 2:
                        sync.wait_ge(c.sem(P + "cq"), fg - 1)
                    for h in range(2):
                        sync.dma_start(w1b[fg % 2][:, h * 8 * 512:(h + 1) * 8 * 512],
                                       chunked_src(io["w1t"], h * 1024, 1024, fg * 512, 512, DFF)
                                       ).then_inc(c.sem(P + "cw" + str(fg % 2)), 16)
                else:
                    if fg >= 2:
                        sync.wait_ge(c.sem(P + "cmm"), (fg - 1) * NTC)
                    for h in range(2):
                        sync.dma_start(iw1[fg % 2][:, h * 8 * 512:(h + 1) * 8 * 512],
                                       chunked_src(scr["iw1"], h * 1024, 1024, fg * 512, 512, DFF)
                                       ).then_inc(c.sem(P + "cw" + str(fg % 2)), 16)

            @block.sync
            def _(sync):
                sync.wait_ge(c.sem(P + "ast"), 16 * NTC)
                ld_w1(sync, 0)
                ld_w1(sync, 1)
                for dc in range(NDC):
                    sync.dma_start(iqf[:, dc * T:(dc + 1) * T],
                                   bass.AP(scr["iq"].tensor, dc * 128, [[D, T], [1, 128]]),
                                   transpose=True).then_inc(c.sem(P + "ctl"), 16)
                for fg in range(2, NFG):
                    ld_w1(sync, fg)

            @block.gpsimd
            def _(gp):
                for fg in range(NFG):
                    if call == 1:
                        gp.wait_ge(c.sem(P + "cq"), fg + 1)
                        gp.dma_start(chunked_src(scr["iw1"], 0, D, fg * 512, 512, DFF),
                                     iw1[fg % 2][:]).then_inc(c.sem(P + "csp"), 16)
                    for tc in range(NTC):
                        k = fg * NTC + tc
                        gp.wait_ge(c.sem(P + "cev"), k + 1)
                        gp.dma_start(rows_src(scr["h"], tc * 128, 128, 512, DFF, fg * 512),
                                     hbuf[k % 4][:]).then_inc(c.sem(P + "cst"), 16)

            def _emit_evac_sc(scalar, fg):
                for tc in range(NTC):
                    k = fg * NTC + tc
                    scalar.wait_ge(c.sem(P + "cmm"), k + 1)
                    if k >= 4:
                        scalar.wait_ge(c.sem(P + "cst"), 16 * (k - 3))
                        scalar.wait_ge(c.sem(P + "crm"), k - 3)
                    scalar.activation(hbuf[k % 4][:], pmm[k % 4][:], AF.Silu,
                                      scale=g["dq1"][:, tc:tc + 1]
                                      ).then_inc(c.sem(P + "cev"), 1)

            def _emit_rm_v(vector, fg):
                for tc in range(NTC):
                    k = fg * NTC + tc
                    vector.wait_ge(c.sem(P + "cev"), k + 1)
                    vector.reduce_max(g["hmax"][:, tc * NFG + fg: tc * NFG + fg + 1],
                                      hbuf[k % 4][:], AX, apply_absolute_value=True
                                      ).then_inc(c.sem(P + "crm"), 1)

            if call == 1:
                @block.scalar
                def _(scalar):
                    for fg in range(NFG):
                        if fg >= 1:
                            _emit_evac_sc(scalar, fg - 1)
                        scalar.wait_ge(c.sem(P + "cw" + str(fg % 2)), 32 * (fg // 2 + 1))
                        for dc in range(NDC):
                            idx = fg * NDC + dc
                            if idx >= 2:
                                scalar.wait_ge(c.sem(P + "cqv"), idx - 1)
                            scalar.activation(q16w[dc % 2][:],
                                              w1b[fg % 2][:, dc * 512:(dc + 1) * 512],
                                              AF.Identity, scale=g["sw1"][:, 0:1],
                                              bias=g["c1536"][:, 0:1]
                                              ).then_inc(c.sem(P + "cqs"), 1)
                    _emit_evac_sc(scalar, NFG - 1)

                @block.vector
                def _(vector):
                    for fg in range(NFG):
                        if fg >= 1:
                            _emit_rm_v(vector, fg - 1)
                        if fg >= 2:
                            vector.wait_ge(c.sem(P + "csp"), 16 * (fg - 1))
                            vector.wait_ge(c.sem(P + "cmm"), (fg - 1) * NTC)
                        for dc in range(NDC):
                            vector.wait_ge(c.sem(P + "cqs"), fg * NDC + dc + 1)
                            sl = iw1[fg % 2][:, dc * 512:(dc + 1) * 512]
                            vector.tensor_scalar(sl, q16w[dc % 2][:], -1536.0, 1.0,
                                                 ALU.add, ALU.min)
                            vector.tensor_scalar_max(sl, sl, -1.0
                                                     ).then_inc(c.sem(P + "cqv"), 1)
                            if dc == NDC - 1:
                                vector.nop().then_inc(c.sem(P + "cq"), 1)
                    _emit_rm_v(vector, NFG - 1)
            else:
                @block.scalar
                def _(scalar):
                    for fg in range(NFG):
                        _emit_evac_sc(scalar, fg)

                @block.vector
                def _(vector):
                    for fg in range(NFG):
                        _emit_rm_v(vector, fg)

            @block.tensor
            def _(tensor):
                tensor.wait_ge(c.sem(P + "ctl"), 16 * NDC)
                for fg in range(NFG):
                    if call == 1:
                        tensor.wait_ge(c.sem(P + "cq"), fg + 1)
                    else:
                        tensor.wait_ge(c.sem(P + "cw" + str(fg % 2)), 32 * (fg // 2 + 1))
                    for tc in range(NTC):
                        k = fg * NTC + tc
                        if k >= 4:
                            tensor.wait_ge(c.sem(P + "cev"), k - 3)
                        for dc in range(NDC):
                            ins = tensor.matmul(
                                pmm[k % 4][:],
                                iqf[:, dc * T + tc * 128: dc * T + (tc + 1) * 128],
                                iw1[fg % 2][:, dc * 512:(dc + 1) * 512],
                                start=(dc == 0), stop=(dc == NDC - 1))
                        ins.then_inc(c.sem(P + "cmm"), 1)

    with contextlib.ExitStack() as ps:
        hcb = [ps.enter_context(nc.sbuf_tensor(f"{P}e_h{i}", [128, DFF], F32)) for i in range(2)]
        q16h = ps.enter_context(nc.sbuf_tensor(f"{P}e_q16", [128, DFF], F16))
        ihb = [ps.enter_context(nc.sbuf_tensor(f"{P}e_ih{i}", [128, DFF], BF16)) for i in range(2)]
        cl2 = ps.enter_context(nc.sbuf_tensor(f"{P}e_col", [128, 4], F32))
        with nc.Block(f"{P}e") as block:
            @block.sync
            def _(sync):
                sync.wait_ge(c.sem(P + "cst"), 16 * NFG * NTC)
                for tc in range(NTC):
                    if tc >= 2:
                        sync.wait_ge(c.sem(P + "dv"), tc - 1)
                    for h in range(2):
                        sync.dma_start(hcb[tc % 2][:, h * (DFF // 2):(h + 1) * (DFF // 2)],
                                       rows_src(scr["h"], tc * 128, 128, DFF // 2, DFF,
                                                h * (DFF // 2))
                                       ).then_inc(c.sem(P + "dh" + str(tc % 2)), 16)

            @block.vector
            def _(vector):
                G = "gv"
                for tc in range(NTC):
                    vector.wait_ge(c.sem(P + "dh" + str(tc % 2)), 32 * (tc // 2 + 1))
                    if tc >= 2:
                        vector.wait_ge(c.sem(P + "dst"), 16 * (tc - 1))
                    c.guard(vector, G, vector.reduce_max(
                        cl2[:, 0:1], g["hmax"][:, tc * NFG:(tc + 1) * NFG], AX))
                    c.guard(vector, G, vector.tensor_scalar_max(cl2[:, 0:1], cl2[:, 0:1], 1e-5))
                    c.guard(vector, G, vector.tensor_scalar_mul(cl2[:, 1:2], cl2[:, 0:1], 0.5 / 127.0))
                    vector.tensor_mul(g["dq2h"][:, tc:tc + 1], cl2[:, 1:2], g["rw2"][:, 0:1])
                    c.guard(vector, G, vector.reciprocal(cl2[:, 2:3], cl2[:, 0:1]))
                    c.guard(vector, G, vector.tensor_scalar_mul(cl2[:, 2:3], cl2[:, 2:3], 127.0))
                    vector.tensor_scalar(q16h[:], hcb[tc % 2][:], cl2[:, 2:3], 1536.0,
                                         ALU.mult, ALU.add)
                    vector.tensor_scalar_add(ihb[tc % 2][:], q16h[:], -1536.0
                                             ).then_inc(c.sem(P + "dv"), 1)

            @block.gpsimd
            def _(gp):
                for tc in range(NTC):
                    gp.wait_ge(c.sem(P + "dv"), tc + 1)
                    gp.dma_start(rows_src(scr["ih"], tc * 128, 128, DFF, DFF), ihb[tc % 2][:]
                                 ).then_inc(c.sem(P + "dst"), 16)

    NGRP = 2
    TCG = NTC // NGRP
    TL = T // NGRP
    NFCG = 8
    with contextlib.ExitStack() as ps:
        ihf = [ps.enter_context(nc.sbuf_tensor(f"{P}g_ihf", [128, NFC * TL], BF16))] * 2
        x0g = ps.enter_context(nc.sbuf_tensor(f"{P}g_x0", [128, TCG * 512], F32))
        ob = [ps.enter_context(nc.sbuf_tensor(f"{P}g_o{i}", [128, 512], F32)) for i in range(4)]
        iw2 = [ps.enter_context(nc.sbuf_tensor(f"{P}g_iw2{i}", [128, 8 * 512], BF16))
               for i in range(2)]
        if call == 1:
            w2b = [ps.enter_context(nc.sbuf_tensor(f"{P}g_w2{i}", [128, 8 * 512], F32))
                   for i in range(2)]
            q16w2 = [ps.enter_context(nc.sbuf_tensor(f"{P}g_q16w{i}", [128, 512], F16))
                     for i in range(2)]
        pdm = [ps.enter_context(nc.psum_tensor(f"{P}g_ps{i}", [128, 512], F32))
               for i in range(TCG)]

        with nc.Block(f"{P}g") as block:
            @block.sync
            def _(sync):
                for grp in range(NGRP):
                    sync.wait_ge(c.sem(P + "dst"), 16 * (grp + 1) * TCG)
                    if grp > 0:
                        sync.wait_ge(c.sem(P + "gmm"), grp * 4 * NFCG)
                    for fc in range(NFC):
                        sync.dma_start(ihf[grp % 2][:, fc * TL:(fc + 1) * TL],
                                       bass.AP(scr["ih"].tensor, grp * TL * DFF + fc * 128,
                                               [[DFF, TL], [1, 128]]),
                                       transpose=True).then_inc(c.sem(P + "gtl"), 16)
                    for dg in range(4):
                        for fcg in range(NFCG):
                            j = (grp * 4 + dg) * NFCG + fcg
                            if call == 1 and grp == 0:
                                if j >= 2:
                                    sync.wait_ge(c.sem(P + "gq"), j - 1)
                                for h in range(2):
                                    sync.dma_start(w2b[j % 2][:, h * 4 * 512:(h + 1) * 4 * 512],
                                                   chunked_src(io["w2t"], fcg * 1024 + h * 512, 512,
                                                               dg * 512, 512, D)
                                                   ).then_inc(c.sem(P + "gw" + str(j % 2)), 16)
                            else:
                                if j >= 2:
                                    sync.wait_ge(c.sem(P + "gmm"), j - 1)
                                if call == 1:
                                    sync.wait_ge(c.sem(P + "gsp"),
                                                 16 * (dg * NFCG + fcg + 1))
                                for h in range(2):
                                    sync.dma_start(iw2[j % 2][:, h * 4 * 512:(h + 1) * 4 * 512],
                                                   chunked_src(scr["iw2"], fcg * 1024 + h * 512, 512,
                                                               dg * 512, 512, D)
                                                   ).then_inc(c.sem(P + "gw" + str(j % 2)), 16)
                        gd = grp * 4 + dg
                        if gd >= 1:
                            sync.wait_ge(c.sem(P + "gev"), gd * TCG)
                        sync.dma_start(x0g[:],
                                       chunked_src(x_src, grp * TCG * 128, TCG * 128,
                                                   dg * 512, 512, D)
                                       ).then_inc(c.sem(P + "gx"), 16)

            def _emit_gevac_v(vector, grp, dg):
                gd = grp * 4 + dg
                for i in range(TCG):
                    tc = grp * TCG + i
                    ke = gd * TCG + i
                    vector.wait_ge(c.sem(P + "gmm"), (gd + 1) * NFCG)
                    vector.wait_ge(c.sem(P + "gx"), 16 * (gd + 1))
                    if ke >= 4:
                        vector.wait_ge(c.sem(P + "gost"), 16 * (ke - 3))
                    vector.scalar_tensor_tensor(
                        ob[ke % 4][:], pdm[i][:], g["dq2h"][:, tc:tc + 1],
                        x0g[:, i * 512:(i + 1) * 512],
                        ALU.mult, ALU.add).then_inc(c.sem(P + "gev"), 1)

            if call == 1:
                @block.scalar
                def _(scalar):
                    for dg in range(4):
                        for fcg in range(NFCG):
                            j = dg * NFCG + fcg
                            scalar.wait_ge(c.sem(P + "gw" + str(j % 2)), 32 * (j // 2 + 1))
                            for cc in range(8):
                                idx = j * 8 + cc
                                if idx >= 2:
                                    scalar.wait_ge(c.sem(P + "gqv"), idx - 1)
                                scalar.activation(q16w2[cc % 2][:],
                                                  w2b[j % 2][:, cc * 512:(cc + 1) * 512],
                                                  AF.Identity, scale=g["sw2"][:, 0:1],
                                                  bias=g["c1536"][:, 0:1]
                                                  ).then_inc(c.sem(P + "gqs"), 1)

                @block.vector
                def _(vector):
                    for dg in range(4):
                        if dg >= 1:
                            _emit_gevac_v(vector, 0, dg - 1)
                        for fcg in range(NFCG):
                            j = dg * NFCG + fcg
                            if j >= 2:
                                vector.wait_ge(c.sem(P + "gsp"), 16 * (j - 1))
                                vector.wait_ge(c.sem(P + "gmm"), j - 1)
                            for cc in range(8):
                                vector.wait_ge(c.sem(P + "gqs"), j * 8 + cc + 1)
                                sl = iw2[j % 2][:, cc * 512:(cc + 1) * 512]
                                vector.tensor_scalar(sl, q16w2[cc % 2][:], -1536.0, 1.0,
                                                     ALU.add, ALU.min)
                                vector.tensor_scalar_max(sl, sl, -1.0
                                                         ).then_inc(c.sem(P + "gqv"), 1)
                                if cc == 7:
                                    vector.nop().then_inc(c.sem(P + "gq"), 1)
                    _emit_gevac_v(vector, 0, 3)
                    for grp in range(1, NGRP):
                        for dg in range(4):
                            _emit_gevac_v(vector, grp, dg)

                @block.gpsimd
                def _(gp):
                    def _stores(grp, dg):
                        for i in range(TCG):
                            tc = grp * TCG + i
                            ke = (grp * 4 + dg) * TCG + i
                            gp.wait_ge(c.sem(P + "gev"), ke + 1)
                            gp.dma_start(rows_src(x_dst, tc * 128, 128, 512, D, dg * 512),
                                         ob[ke % 4][:]).then_inc(c.sem(P + "gost"), 16)
                    for dg in range(4):
                        if dg >= 1:
                            _stores(0, dg - 1)
                        for fcg in range(NFCG):
                            j = dg * NFCG + fcg
                            gp.wait_ge(c.sem(P + "gq"), j + 1)
                            gp.dma_start(chunked_src(scr["iw2"], fcg * 1024, 1024,
                                                     dg * 512, 512, D),
                                         iw2[j % 2][:]).then_inc(c.sem(P + "gsp"), 16)
                    _stores(0, 3)
                    for grp in range(1, NGRP):
                        for dg in range(4):
                            _stores(grp, dg)
            else:
                @block.vector
                def _(vector):
                    for grp in range(NGRP):
                        for dg in range(4):
                            _emit_gevac_v(vector, grp, dg)

            @block.tensor
            def _(tensor):
                for grp in range(NGRP):
                    tensor.wait_ge(c.sem(P + "gtl"), 16 * (grp + 1) * NFC)
                    for dg in range(4):
                        gd = grp * 4 + dg
                        for fcg in range(NFCG):
                            j = gd * NFCG + fcg
                            if call == 1 and grp == 0:
                                tensor.wait_ge(c.sem(P + "gq"), j + 1)
                            else:
                                tensor.wait_ge(c.sem(P + "gw" + str(j % 2)), 32 * (j // 2 + 1))
                            for i in range(TCG):
                                if fcg == 0 and gd >= 1:
                                    tensor.wait_ge(c.sem(P + "gev"), (gd - 1) * TCG + i + 1)
                                for fi in range(8):
                                    fc = fcg * 8 + fi
                                    ins = tensor.matmul(
                                        pdm[i][:],
                                        ihf[grp % 2][:, fc * TL + i * 128: fc * TL + (i + 1) * 128],
                                        iw2[j % 2][:, fi * 512:(fi + 1) * 512],
                                        start=(fcg == 0 and fi == 0),
                                        stop=(fcg == NFCG - 1 and fi == 7))
                            ins.then_inc(c.sem(P + "gmm"), 1)

            if call != 1:
                @block.gpsimd
                def _(gp):
                    for grp in range(NGRP):
                        for dg in range(4):
                            for i in range(TCG):
                                tc = grp * TCG + i
                                ke = (grp * 4 + dg) * TCG + i
                                gp.wait_ge(c.sem(P + "gev"), ke + 1)
                                gp.dma_start(rows_src(x_dst, tc * 128, 128, 512, D, dg * 512),
                                             ob[ke % 4][:]).then_inc(c.sem(P + "gost"), 16)
    c.cnt[P + "gost"] = 16 * NGRP * 4 * TCG



def _emit_mamba(c, io, g, scr):
    nc = c.nc
    x_src, x_dst = io["x_src"], io["x_dst"]
    P = "m"
    TCH = 256
    NCH = T // TCH
    NTC2 = TCH // 128

    with contextlib.ExitStack() as outer:

        with contextlib.ExitStack() as ps:
            xfh = ps.enter_context(nc.sbuf_tensor("ma_xfh", [128, NDC * T], BF16))
            xfl = ps.enter_context(nc.sbuf_tensor("ma_xfl", [128, NDC * T], BF16))
            _emit_mamba_a(c, g, x_src, xfh, xfl)
            _emit_mamba_b(c, g, io, scr, xfh, xfl)
        _emit_mamba_c(c, g, io, scr)
        _emit_mamba_d(c, g, io, scr)
        _emit_mamba_d2(c, g, io, scr)
        _emit_mamba_e(c, g, io, scr, x_src, x_dst, TCH, NCH, NTC2)
        c.cnt["meost"] = 16 * NCH * 4 * NTC2


def _emit_mamba_a(c, g, x_src, xfh, xfl):
    nc = c.nc
    P = "ma"
    with contextlib.ExitStack() as ps:
        xa = [ps.enter_context(nc.sbuf_tensor(f"ma_x{i}", [128, D], F32)) for i in range(2)]
        sqd = ps.enter_context(nc.sbuf_tensor("ma_sqd", [128, D], F32))
        cl = ps.enter_context(nc.sbuf_tensor("ma_col", [128, 8], F32))
        diag = [ps.enter_context(nc.sbuf_tensor(f"ma_dg{i}", [128, 128], F32)) for i in range(2)]
        nmr = [ps.enter_context(nc.sbuf_tensor(f"ma_nm{i}", [1, 128], F32)) for i in range(2)]
        pt = [ps.enter_context(nc.psum_tensor(f"ma_pt{i}", [128, 128], F32)) for i in range(4)]
        pr = ps.enter_context(nc.psum_tensor("ma_pr", [1, 128], F32))
        with nc.Block("ma") as block:
            @block.sync
            def _(sync):
                for tc in range(NTC):
                    if tc >= 2:
                        sync.wait_ge(c.sem(P + "ev"), 16 * (tc - 1))
                    sync.dma_start(xa[tc % 2][:], rows_src(x_src, tc * 128, 128, D, D)
                                   ).then_inc(c.sem(P + "x"), 16)

            @block.scalar
            def _(scalar):
                for tc in range(NTC):
                    scalar.wait_ge(c.sem(P + "x"), 16 * (tc + 1))
                    scalar.activation(sqd[:], xa[tc % 2][:], AF.Square
                                      ).then_inc(c.sem(P + "s"), 1)
                    scalar.wait_ge(c.sem(P + "v"), 3 * tc + 1)
                    scalar.activation(cl[:, 4:5], cl[:, 3:4], AF.Sqrt,
                                      bias=g["ceps"][:, 0:1]).then_inc(c.sem(P + "s"), 1)

            def _ma_evacs(vector, tc):
                for dc in range(NDC):
                    vector.wait_ge(c.sem(P + "tm"), tc * 17 + 1 + dc + 1)
                    dst_h = xfh[:, dc * T + tc * 128: dc * T + (tc + 1) * 128]
                    dst_l = xfl[:, dc * T + tc * 128: dc * T + (tc + 1) * 128]
                    vector.tensor_copy(dst_h, pt[dc % 4][:])
                    vector.tensor_sub(dst_l, pt[dc % 4][:], dst_h
                                      ).then_inc(c.sem(P + "ev"), 1)

            @block.vector
            def _(vector):
                G = "gv"
                for tc in range(NTC):
                    vector.wait_ge(c.sem(P + "x"), 16 * (tc + 1))
                    c.guard(vector, G, vector.reduce_sum(cl[:, 0:1], xa[tc % 2][:], AX))
                    vector.wait_ge(c.sem(P + "s"), 2 * tc + 1)
                    c.guard(vector, G, vector.reduce_sum(cl[:, 1:2], sqd[:], AX))
                    c.guard(vector, G, vector.tensor_scalar_mul(cl[:, 2:3], cl[:, 0:1], 1.0 / D))
                    c.guard(vector, G, vector.tensor_scalar_mul(cl[:, 3:4], cl[:, 1:2], 1.0 / D))
                    c.guard(vector, G, vector.tensor_mul(cl[:, 0:1], cl[:, 2:3], cl[:, 2:3]))
                    vector.tensor_sub(cl[:, 3:4], cl[:, 3:4], cl[:, 0:1]
                                      ).then_inc(c.sem(P + "v"), 1)
                    vector.wait_ge(c.sem(P + "s"), 2 * tc + 2)
                    c.guard(vector, G, vector.reciprocal(cl[:, 4:5], cl[:, 4:5]))
                    c.guard(vector, G, vector.tensor_mul(cl[:, 5:6], cl[:, 2:3], cl[:, 4:5]))
                    c.guard(vector, G, vector.tensor_scalar_mul(cl[:, 5:6], cl[:, 5:6], -1.0))
                    vector.tensor_scalar_mul(diag[tc % 2][:], g["eye"][:], cl[:, 4:5]
                                             ).then_inc(c.sem(P + "v"), 1)
                    if tc >= 1:
                        _ma_evacs(vector, tc - 1)
                    vector.wait_ge(c.sem(P + "tm"), tc * 17 + 1)
                    vector.tensor_copy(nmr[tc % 2][:], pr[0:1, :]).then_inc(c.sem(P + "v"), 1)
                _ma_evacs(vector, NTC - 1)

            @block.tensor
            def _(tensor):
                for tc in range(NTC):
                    tensor.wait_ge(c.sem(P + "v"), 3 * tc + 2)
                    tensor.transpose(pr[:], cl[:, 5:6], g["eye"][:]).then_inc(c.sem(P + "tm"), 1)
                    tensor.wait_ge(c.sem(P + "v"), 3 * tc + 3)
                    for dc in range(NDC):
                        gi = tc * NDC + dc
                        if gi >= 4:
                            tensor.wait_ge(c.sem(P + "ev"), gi - 3)
                        tensor.matmul(pt[dc % 4][:], xa[tc % 2][:, dc * 128:(dc + 1) * 128],
                                      diag[tc % 2][:], start=True, stop=False)
                        tensor.matmul(pt[dc % 4][:], g["ones1"][:], nmr[tc % 2][:],
                                      start=False, stop=True).then_inc(c.sem(P + "tm"), 1)


def _emit_mamba_b(c, g, io, scr, xfh, xfl):
    nc = c.nc
    P = "mb"
    NOC = (2 * DIN) // 128
    with contextlib.ExitStack() as ps:
        wb = [ps.enter_context(nc.sbuf_tensor(f"mb_w{i}", [128, NDC * 128], F32)) for i in range(2)]
        wh = [ps.enter_context(nc.sbuf_tensor(f"mb_wh{i}", [128, NDC * 128], BF16)) for i in range(2)]
        wl = [ps.enter_context(nc.sbuf_tensor(f"mb_wl{i}", [128, NDC * 128], BF16)) for i in range(2)]
        evh = [ps.enter_context(nc.sbuf_tensor(f"mb_eh{i}", [128, 512], BF16)) for i in range(4)]
        evl = [ps.enter_context(nc.sbuf_tensor(f"mb_el{i}", [128, 512], BF16)) for i in range(4)]
        evs = [ps.enter_context(nc.sbuf_tensor(f"mb_es{i}", [128, 512], F32)) for i in range(4)]
        pb = [ps.enter_context(nc.psum_tensor(f"mb_ps{i}", [128, 512], F32)) for i in range(2)]
        with nc.Block("mb") as block:
            @block.sync
            def _(sync):
                for oc in range(NOC):
                    if oc >= 2:
                        sync.wait_ge(c.sem(P + "q"), oc - 1)
                    sync.dma_start(wb[oc % 2][:],
                                   chunked_src(io["w_in"], 0, D, oc * 128, 128, 2 * DIN)
                                   ).then_inc(c.sem(P + "w"), 16)

            @block.vector
            def _(vector):
                for oc in range(NOC):
                    vector.wait_ge(c.sem(P + "w"), 16 * (oc + 1))
                    if oc >= 2:
                        vector.wait_ge(c.sem(P + "mm"), (oc - 1) * 4)
                    vector.tensor_copy(wh[oc % 2][:], wb[oc % 2][:])
                    vector.tensor_sub(wl[oc % 2][:], wb[oc % 2][:], wh[oc % 2][:]
                                      ).then_inc(c.sem(P + "q"), 1)
                    if oc >= 1:
                        _mb_evac_v(c, vector, oc - 1, pb, evh, evl)
                _mb_evac_v(c, vector, NOC - 1, pb, evh, evl)

            @block.scalar
            def _(scalar):
                for oc in range(16, NOC):
                    for tg in range(4):
                        k = oc * 4 + tg
                        kr = (oc - 16) * 4 + tg
                        scalar.wait_ge(c.sem(P + "mm"), k + 1)
                        if kr >= 4:
                            scalar.wait_ge(c.sem(P + "st"), 16 * (128 + kr - 3))
                        scalar.activation(evs[tg % 4][:], pb[k % 2][:], AF.Silu
                                          ).then_inc(c.sem(P + "evs"), 1)

            @block.tensor
            def _(tensor):
                for oc in range(NOC):
                    tensor.wait_ge(c.sem(P + "q"), oc + 1)
                    for tg in range(4):
                        k = oc * 4 + tg
                        if k >= 2:
                            tensor.wait_ge(c.sem(P + "ev"), k - 1)
                        for dc in range(NDC):
                            for (lt, rt) in ((wh, xfh), (wh, xfl), (wl, xfh)):
                                ins = tensor.matmul(
                                    pb[k % 2][:],
                                    lt[oc % 2][:, dc * 128:(dc + 1) * 128],
                                    rt[:, dc * T + tg * 512: dc * T + tg * 512 + 512],
                                    start=(dc == 0 and lt is wh and rt is xfh),
                                    stop=(dc == NDC - 1 and lt is wl))
                        ins.then_inc(c.sem(P + "mm"), 1)

            @block.gpsimd
            def _(gp):
                for oc in range(NOC):
                    for tg in range(4):
                        k = oc * 4 + tg
                        if oc < 16:
                            gp.wait_ge(c.sem(P + "ev"), k + 1)
                            gp.dma_start(rows_src(scr["xch"], oc * 128, 128, 512, T, tg * 512),
                                         evh[k % 4][:]).then_inc(c.sem(P + "st"), 16)
                            gp.dma_start(rows_src(scr["xcl"], oc * 128, 128, 512, T, tg * 512),
                                         evl[k % 4][:]).then_inc(c.sem(P + "st"), 16)
                        else:
                            gp.wait_ge(c.sem(P + "evs"), (oc - 16) * 4 + tg + 1)
                            gp.dma_start(rows_src(scr["sr"], (oc - 16) * 128, 128, 512, T, tg * 512),
                                         evs[tg % 4][:]).then_inc(c.sem(P + "st"), 16)


def _mb_evac_v(c, vector, oc, pb, evh, evl):
    P = "mb"
    if oc >= 16:
        for tg in range(4):
            k = oc * 4 + tg
            vector.wait_ge(c.sem(P + "evs"), (oc - 16) * 4 + tg + 1)
            vector.nop().then_inc(c.sem(P + "ev"), 1)
        return
    for tg in range(4):
        k = oc * 4 + tg
        vector.wait_ge(c.sem(P + "mm"), k + 1)
        if k >= 4:
            vector.wait_ge(c.sem(P + "st"), 32 * (k - 3))
        vector.tensor_copy(evh[k % 4][:], pb[k % 2][:])
        vector.tensor_sub(evl[k % 4][:], pb[k % 2][:], evh[k % 4][:]
                          ).then_inc(c.sem(P + "ev"), 1)


def _emit_mamba_c(c, g, io, scr):
    nc = c.nc
    P = "mc"
    TP = T + 4
    with contextlib.ExitStack() as ps:
        xch = ps.enter_context(nc.sbuf_tensor("mc_xh", [128, NDC * TP], BF16))
        xcl = ps.enter_context(nc.sbuf_tensor("mc_xl", [128, NDC * TP], BF16))
        wb = [ps.enter_context(nc.sbuf_tensor(f"mc_w{i}", [128, NDC * 128], F32))
              for i in range(2)]
        whc = ps.enter_context(nc.sbuf_tensor("mc_wh", [128, 5 * NDC * 128], BF16))
        wlc = ps.enter_context(nc.sbuf_tensor("mc_wl", [128, 5 * NDC * 128], BF16))
        exs = [ps.enter_context(nc.sbuf_tensor(f"mc_xs{i}", [128, 512], F32)) for i in range(2)]
        eu = [ps.enter_context(nc.sbuf_tensor(f"mc_u{i}", [128, 512], F32)) for i in range(2)]
        euh = [ps.enter_context(nc.sbuf_tensor(f"mc_uh{i}", [128, 512], BF16)) for i in range(2)]
        eul = [ps.enter_context(nc.sbuf_tensor(f"mc_ul{i}", [128, 512], BF16)) for i in range(2)]
        pc = [ps.enter_context(nc.psum_tensor(f"mc_ps{i}", [128, 512], F32)) for i in range(2)]
        with nc.Block("mc") as block:
            @block.sync
            def _(sync):
                sync.wait_ge(c.sem("mb" + "st"), 16 * (16 * 4 * 2 + 16 * 4))
                for ic in range(NDC):
                    sync.dma_start(xch[:, ic * TP + 3: ic * TP + 3 + T],
                                   rows_src(scr["xch"], ic * 128, 128, T, T)
                                   ).then_inc(c.sem(P + "x"), 16)
                    sync.dma_start(xcl[:, ic * TP + 3: ic * TP + 3 + T],
                                   rows_src(scr["xcl"], ic * 128, 128, T, T)
                                   ).then_inc(c.sem(P + "x"), 16)
                for oc in range(NDC):
                    for k in range(KCV):
                        j = oc * KCV + k
                        if j >= 2:
                            sync.wait_ge(c.sem(P + "q"), j)
                        sync.dma_start(wb[j % 2][:],
                                       chunked_src(io["convw"], k * DIN, DIN, oc * 128, 128, DIN)
                                       ).then_inc(c.sem(P + "w" + str(j % 2)), 16)

            @block.vector
            def _(vector):
                for ic in range(NDC):
                    vector.memset(xch[:, ic * TP: ic * TP + 3], 0.0)
                    vector.memset(xcl[:, ic * TP: ic * TP + 3], 0.0)
                vector.nop().then_inc(c.sem(P + "q"), 1)
                for oc in range(NDC):
                    for k in range(KCV):
                        j = oc * KCV + k
                        vector.wait_ge(c.sem(P + "w" + str(j % 2)), 16 * (j // 2 + 1))
                        if j >= 5:
                            vector.wait_ge(c.sem(P + "mm"), ((j - 5) // KCV + 1) * 4)
                        sl_h = whc[:, (j % 5) * NDC * 128: (j % 5 + 1) * NDC * 128]
                        sl_l = wlc[:, (j % 5) * NDC * 128: (j % 5 + 1) * NDC * 128]
                        vector.tensor_copy(sl_h, wb[j % 2][:])
                        vector.tensor_sub(sl_l, wb[j % 2][:], sl_h).then_inc(c.sem(P + "q"), 1)
                    if oc >= 1:
                        _mc_usplit_v(c, vector, oc - 1, eu, euh, eul)
                _mc_usplit_v(c, vector, NDC - 1, eu, euh, eul)

            @block.tensor
            def _(tensor):
                tensor.wait_ge(c.sem(P + "x"), 16 * 2 * NDC)
                for oc in range(NDC):
                    tensor.wait_ge(c.sem(P + "q"), oc * KCV + KCV + 1)
                    for tg in range(4):
                        kk = oc * 4 + tg
                        if kk >= 2:
                            tensor.wait_ge(c.sem(P + "es"), 2 * (kk - 1))
                        first = True
                        for k in range(KCV):
                            j = oc * KCV + k
                            slot = j % 5
                            for ic in range(NDC):
                                off = ic * TP + tg * 512 + k
                                for (lt, rt) in ((whc, xch), (whc, xcl), (wlc, xch)):
                                    ins = tensor.matmul(
                                        pc[kk % 2][:],
                                        lt[:, slot * NDC * 128 + ic * 128: slot * NDC * 128 + (ic + 1) * 128],
                                        rt[:, off: off + 512],
                                        start=first, stop=(k == KCV - 1 and ic == NDC - 1 and lt is wlc))
                                    first = False
                        ins.then_inc(c.sem(P + "mm"), 1)

            @block.scalar
            def _(scalar):
                for oc in range(NDC):
                    for tg in range(4):
                        k = oc * 4 + tg
                        scalar.wait_ge(c.sem(P + "mm"), k + 1)
                        if k >= 2:
                            scalar.wait_ge(c.sem(P + "st"), 64 * (k - 1))
                            scalar.wait_ge(c.sem(P + "ev"), k - 1)
                        scalar.activation(exs[k % 2][:], pc[k % 2][:], AF.Identity,
                                          bias=g["convb"][:, oc:oc + 1])
                        scalar.activation(eu[k % 2][:], pc[k % 2][:], AF.Silu,
                                          bias=g["convb"][:, oc:oc + 1]
                                          ).then_inc(c.sem(P + "es"), 2)

            @block.gpsimd
            def _(gp):
                for oc in range(NDC):
                    for tg in range(4):
                        k = oc * 4 + tg
                        gp.wait_ge(c.sem(P + "es"), 2 * (k + 1))
                        gp.dma_start(rows_src(scr["xs"], oc * 128, 128, 512, T, tg * 512),
                                     exs[k % 2][:]).then_inc(c.sem(P + "st"), 16)
                        gp.dma_start(rows_src(scr["u"], oc * 128, 128, 512, T, tg * 512),
                                     eu[k % 2][:]).then_inc(c.sem(P + "st"), 16)
                        gp.wait_ge(c.sem(P + "ev"), k + 1)
                        gp.dma_start(rows_src(scr["uh"], oc * 128, 128, 512, T, tg * 512),
                                     euh[k % 2][:]).then_inc(c.sem(P + "st"), 16)
                        gp.dma_start(rows_src(scr["ul"], oc * 128, 128, 512, T, tg * 512),
                                     eul[k % 2][:]).then_inc(c.sem(P + "st"), 16)


def _mc_usplit_v(c, vector, oc, eu, euh, eul):
    P = "mc"
    for tg in range(4):
        k = oc * 4 + tg
        vector.wait_ge(c.sem(P + "es"), 2 * (k + 1))
        if k >= 2:
            vector.wait_ge(c.sem(P + "st"), 64 * (k - 1))
        vector.tensor_copy(euh[k % 2][:], eu[k % 2][:])
        vector.tensor_sub(eul[k % 2][:], eu[k % 2][:], euh[k % 2][:]
                          ).then_inc(c.sem(P + "ev"), 1)


def _emit_mamba_d(c, g, io, scr):
    nc = c.nc
    P = "md"
    with contextlib.ExitStack() as ps:
        bc_sb = ps.enter_context(nc.sbuf_tensor("m_bc", [32, T], F32))
        uh = ps.enter_context(nc.sbuf_tensor("md_uh", [128, NDC * T], BF16))
        ul = ps.enter_context(nc.sbuf_tensor("md_ul", [128, NDC * T], BF16))
        wxb = ps.enter_context(nc.sbuf_tensor("md_wx", [128, NDC * 160], F32))
        wxh = ps.enter_context(nc.sbuf_tensor("md_wxh", [128, NDC * 160], BF16))
        wxl = ps.enter_context(nc.sbuf_tensor("md_wxl", [128, NDC * 160], BF16))
        wdb = ps.enter_context(nc.sbuf_tensor("md_wd", [128, D], F32))
        wdh = ps.enter_context(nc.sbuf_tensor("md_wdh", [128, D], BF16))
        wdl = ps.enter_context(nc.sbuf_tensor("md_wdl", [128, D], BF16))
        dbh = ps.enter_context(nc.sbuf_tensor("md_dbh", [128, T], BF16))
        dbl = ps.enter_context(nc.sbuf_tensor("md_dbl", [128, T], BF16))
        sp = [ps.enter_context(nc.sbuf_tensor(f"md_sp{i}", [128, 512], F32)) for i in range(6)]
        dlt = [ps.enter_context(nc.sbuf_tensor(f"md_dl{i}", [128, 512], F32)) for i in range(4)]
        pdt = ps.enter_context(nc.psum_tensor("md_p0", [128, 512], F32))
        pbc = ps.enter_context(nc.psum_tensor("md_p1", [32, 512], F32))
        pd2 = [ps.enter_context(nc.psum_tensor(f"md_p2{i}", [128, 512], F32)) for i in range(2)]
        with nc.Block("md") as block:
            @block.sync
            def _(sync):
                sync.wait_ge(c.sem("mc" + "st"), 16 * 4 * NDC * 4)
                sync.dma_start(uh[:], chunked_src(scr["uh"], 0, DIN, 0, T, T)
                               ).then_inc(c.sem(P + "x"), 16)
                sync.dma_start(ul[:], chunked_src(scr["ul"], 0, DIN, 0, T, T)
                               ).then_inc(c.sem(P + "x"), 16)
                sync.dma_start(wxb[:], chunked_src(io["wx"], 0, DIN, 0, 160, 160)
                               ).then_inc(c.sem(P + "x"), 16)
                sync.dma_start(wdb[:], rows_src(io["wdt"], 0, 128, D, D)
                               ).then_inc(c.sem(P + "x"), 16)

            @block.vector
            def _(vector):
                vector.wait_ge(c.sem(P + "x"), 16 * 4)
                vector.tensor_copy(wxh[:], wxb[:])
                vector.tensor_sub(wxl[:], wxb[:], wxh[:])
                vector.tensor_copy(wdh[:], wdb[:])
                vector.tensor_sub(wdl[:], wdb[:], wdh[:]).then_inc(c.sem(P + "q"), 1)
                for tg in range(4):
                    vector.wait_ge(c.sem(P + "mm"), tg + 1)
                    vector.tensor_copy(dbh[:, tg * 512:(tg + 1) * 512], pdt[:])
                    vector.tensor_sub(dbl[:, tg * 512:(tg + 1) * 512], pdt[:],
                                      dbh[:, tg * 512:(tg + 1) * 512])
                    vector.tensor_copy(bc_sb[:, tg * 512:(tg + 1) * 512], pbc[:]
                                       ).then_inc(c.sem(P + "ev"), 1)
                for do in range(NDC):
                    for tg in range(4):
                        k = do * 4 + tg
                        vector.wait_ge(c.sem(P + "sp"), 4 * (k + 1))
                        if k >= 4:
                            vector.wait_ge(c.sem(P + "dst"), 16 * (k - 3))
                        vector.tensor_add(dlt[k % 4][:], sp[(k % 2) * 3 + 1][:],
                                          sp[(k % 2) * 3 + 2][:]).then_inc(c.sem(P + "dv"), 1)

            @block.tensor
            def _(tensor):
                tensor.wait_ge(c.sem(P + "q"), 1)
                for tg in range(4):
                    if tg >= 1:
                        tensor.wait_ge(c.sem(P + "ev"), tg)
                    first = True
                    for dc in range(NDC):
                        for (lt, rt) in ((wxh, uh), (wxh, ul), (wxl, uh)):
                            tensor.matmul(pdt[:],
                                          lt[:, dc * 160: dc * 160 + 128],
                                          rt[:, dc * T + tg * 512: dc * T + tg * 512 + 512],
                                          start=first, stop=(dc == NDC - 1 and lt is wxl))
                            first = False
                    first = True
                    for dc in range(NDC):
                        for (lt, rt) in ((wxh, uh), (wxh, ul), (wxl, uh)):
                            ins = tensor.matmul(
                                pbc[:],
                                lt[:, dc * 160 + 128: dc * 160 + 160],
                                rt[:, dc * T + tg * 512: dc * T + tg * 512 + 512],
                                start=first, stop=(dc == NDC - 1 and lt is wxl))
                            first = False
                    ins.then_inc(c.sem(P + "mm"), 1)
                tensor.wait_ge(c.sem(P + "ev"), 4)
                for do in range(NDC):
                    for tg in range(4):
                        k = do * 4 + tg
                        if k >= 2:
                            tensor.wait_ge(c.sem(P + "sp"), 4 * (k - 2) + 2)
                        first = True
                        for (lt, rt) in ((wdh, dbh), (wdh, dbl), (wdl, dbh)):
                            ins = tensor.matmul(
                                pd2[k % 2][:],
                                lt[:, do * 128:(do + 1) * 128],
                                rt[:, tg * 512:(tg + 1) * 512],
                                start=first, stop=(lt is wdl))
                            first = False
                        ins.then_inc(c.sem(P + "m2"), 1)

            @block.scalar
            def _(scalar):
                for do in range(NDC):
                    for tg in range(4):
                        k = do * 4 + tg
                        scalar.wait_ge(c.sem(P + "m2"), k + 1)
                        if k >= 4:
                            scalar.wait_ge(c.sem(P + "dst"), 16 * (k - 3))
                        b = (k % 2) * 3
                        scalar.activation(sp[b + 0][:], pd2[k % 2][:], AF.Abs,
                                          bias=g["bdt"][:, do:do + 1]).then_inc(c.sem(P + "sp"), 1)
                        scalar.activation(sp[b + 1][:], pd2[k % 2][:], AF.Relu,
                                          bias=g["bdt"][:, do:do + 1]).then_inc(c.sem(P + "sp"), 1)
                        scalar.activation(sp[b + 2][:], sp[b + 0][:], AF.Exp, scale=-1.0
                                          ).then_inc(c.sem(P + "sp"), 1)
                        scalar.activation(sp[b + 2][:], sp[b + 2][:], AF.Ln, bias=1.0
                                          ).then_inc(c.sem(P + "sp"), 1)

            @block.gpsimd
            def _(gp):
                gp.wait_ge(c.sem(P + "ev"), 4)
                gp.dma_start(rows_src(scr["bc"], 0, 32, T, T), bc_sb[:]
                             ).then_inc(c.sem(P + "dst"), 16)
                for do in range(NDC):
                    for tg in range(4):
                        k = do * 4 + tg
                        gp.wait_ge(c.sem(P + "dv"), k + 1)
                        gp.dma_start(rows_src(scr["delta"], do * 128, 128, 512, T, tg * 512),
                                     dlt[k % 4][:]).then_inc(c.sem(P + "dst"), 16)


def _emit_mamba_d2(c, g, io, scr):
    nc = c.nc
    P = "md"
    with contextlib.ExitStack() as ps:
        wob = [ps.enter_context(nc.sbuf_tensor(f"m2_wo{i}", [128, NDC * 512], F32)) for i in range(2)]
        woh = [ps.enter_context(nc.sbuf_tensor(f"m2_wh{i}", [128, NDC * 512], BF16)) for i in range(2)]
        wol = [ps.enter_context(nc.sbuf_tensor(f"m2_wl{i}", [128, NDC * 512], BF16)) for i in range(2)]
        with nc.Block("md2") as block:
            @block.sync
            def _(sync):
                for dg in range(4):
                    sync.wait_ge(c.sem(P + "q"), dg + 1)
                    sync.dma_start(wob[dg % 2][:],
                                   chunked_src(io["wout"], 0, D, dg * 512, 512, D)
                                   ).then_inc(c.sem(P + "x"), 16)

            @block.vector
            def _(vector):
                for dg in range(4):
                    vector.wait_ge(c.sem(P + "x"), 16 * (4 + dg + 1))
                    if dg >= 2:
                        vector.wait_ge(c.sem(P + "wst"), 32 * (dg - 1))
                    vector.tensor_copy(woh[dg % 2][:], wob[dg % 2][:])
                    vector.tensor_sub(wol[dg % 2][:], wob[dg % 2][:], woh[dg % 2][:]
                                      ).then_inc(c.sem(P + "q"), 1)

            @block.gpsimd
            def _(gp):
                for dg in range(4):
                    gp.wait_ge(c.sem(P + "q"), dg + 2)
                    gp.dma_start(chunked_src(scr["wouth"], 0, D, dg * 512, 512, D),
                                 woh[dg % 2][:]).then_inc(c.sem(P + "wst"), 16)
                    gp.dma_start(chunked_src(scr["woutl"], 0, D, dg * 512, 512, D),
                                 wol[dg % 2][:]).then_inc(c.sem(P + "wst"), 16)


def _emit_mamba_e(c, g, io, scr, x_src, x_dst, TCH, NCH, NTC2):
    nc = c.nc
    P = "me"
    NW = NDC * TCH
    with contextlib.ExitStack() as ps:
        deltab = [ps.enter_context(nc.sbuf_tensor(f"me_dl{i}", [128, NW], F32)) for i in range(2)]
        ub = [ps.enter_context(nc.sbuf_tensor(f"me_ub{i}", [128, NW], F32)) for i in range(2)]
        srb = ps.enter_context(nc.sbuf_tensor("me_sr", [128, NW], F32))
        xsb = ps.enter_context(nc.sbuf_tensor("me_xs", [128, NW], F32))
        dub = ps.enter_context(nc.sbuf_tensor("me_du", [128, NW], F32))
        yb = ps.enter_context(nc.sbuf_tensor("me_y", [128, NW], F32))
        ytmp = ps.enter_context(nc.sbuf_tensor("me_yt", [128, TCH], F32))
        ysr = dub
        yh = ps.enter_context(nc.sbuf_tensor("me_yh", [128, NW], BF16))
        yl = ps.enter_context(nc.sbuf_tensor("me_yl", [128, NW], BF16))
        dab = [ps.enter_context(nc.sbuf_tensor(f"me_da{i}", [128, TCH], F32)) for i in range(4)]
        dbub = ps.enter_context(nc.sbuf_tensor("me_db", [128, 4 * TCH], F32))
        hb = [ps.enter_context(nc.sbuf_tensor(f"me_h{i}", [128, TCH], F32)) for i in range(4)]
        brepb = [ps.enter_context(nc.sbuf_tensor(f"me_br{i}", [128, TCH], F32)) for i in range(2)]
        crepb = [ps.enter_context(nc.sbuf_tensor(f"me_cr{i}", [128, TCH], F32)) for i in range(2)]
        wohb = ps.enter_context(nc.sbuf_tensor("me_woh", [128, NDC * 512], BF16))
        wolb = ps.enter_context(nc.sbuf_tensor("me_wol", [128, NDC * 512], BF16))
        ob = [ps.enter_context(nc.sbuf_tensor(f"me_o{i}", [128, 512], F32)) for i in range(4)]
        pw = [ps.enter_context(nc.psum_tensor(f"me_pw{i}", [128, 512], F32)) for i in range(2)]

        NT = NST * NDC

        def ld_delta_u(sync, ch):
            lp = c.sem(P + "ld%d" % (ch % 2))
            if ch >= 2:
                sync.wait_ge(c.sem(P + "da"), (ch - 1) * NT)
                sync.wait_ge(c.sem(P + "du"), ch - 1)
                sync.wait_ge(c.sem(P + "ost"), 16 * 8 * (ch - 1))
            sync.dma_start(deltab[ch % 2][:],
                           chunked_src(scr["delta"], 0, DIN, ch * TCH, TCH, T)
                           ).then_inc(lp, 16)
            if ch >= 2:
                sync.wait_ge(c.sem(P + "fin"), ch - 1)
            sync.dma_start(ub[ch % 2][:], chunked_src(scr["u"], 0, DIN, ch * TCH, TCH, T)
                           ).then_inc(lp, 16)

        def ld_wl(sync, ch, dg):
            j = ch * 4 + dg
            if j >= 1:
                sync.wait_ge(c.sem(P + "wmm"), j * NTC2)
            sync.dma_start(wohb[:], chunked_src(scr["wouth"], 0, D, dg * 512, 512, D)
                           ).then_inc(c.sem(P + "wl"), 16)
            sync.dma_start(wolb[:], chunked_src(scr["woutl"], 0, D, dg * 512, 512, D)
                           ).then_inc(c.sem(P + "wl"), 16)

        with nc.Block("me") as block:
            @block.sync
            def _(sync):
                sync.wait_ge(c.sem("md" + "dst"), 16 * (NDC * 4 + 1))
                sync.wait_ge(c.sem("md" + "wst"), 16 * 8)
                sync.wait_ge(c.sem("mc" + "st"), 16 * 4 * NDC * 4)
                ld_delta_u(sync, 0)
                for ch in range(NCH):
                    if ch >= 1:
                        sync.wait_ge(c.sem(P + "wmm"), ch * 4 * NTC2)
                    sync.dma_start(srb[:], chunked_src(scr["sr"], 0, DIN, ch * TCH, TCH, T)
                                   ).then_inc(c.sem(P + "ld"), 16)
                    sync.dma_start(xsb[:], chunked_src(scr["xs"], 0, DIN, ch * TCH, TCH, T)
                                   ).then_inc(c.sem(P + "ld"), 16)
                    ld_wl(sync, ch, 0)
                    if ch + 1 < NCH:
                        ld_delta_u(sync, ch + 1)
                    sync.wait_ge(c.sem(P + "da"), (ch + 1) * NT)
                    sync.wait_ge(c.sem(P + "du"), ch + 1)
                    sync.dma_start(deltab[ch % 2][:],
                                   chunked_src(x_src, ch * TCH, TCH, 0, D, D)
                                   ).then_inc(c.sem(P + "sel"), 16)
                    for dg in range(1, 4):
                        ld_wl(sync, ch, dg)

            def evac(scalar, ch2, e0):
                e = ch2 * 4 * NTC2 + e0
                scalar.wait_ge(c.sem(P + "wmm"), e + 1)
                if e >= 4:
                    scalar.wait_ge(c.sem(P + "ost"), 16 * (e - 3))
                scalar.activation(ob[e % 4][:], pw[e0 % 2][:], AF.Identity
                                  ).then_inc(c.sem(P + "ev"), 1)

            @block.scalar
            def _(scalar):
                for ch in range(NCH):
                    scalar.wait_ge(c.sem(P + "ld%d" % (ch % 2)), 16 * 2 * (ch // 2 + 1))
                    for n in range(NST):
                        for dc in range(NDC):
                            t = n * NDC + dc
                            if ch >= 1 and t % 16 == 0:
                                if 1 <= t // 16 <= 4:
                                    evac(scalar, ch - 1, t // 16 - 1)
                                elif t // 16 >= 12:
                                    evac(scalar, ch - 1, t // 16 - 8)
                            idx = ch * NT + t
                            if idx >= 4:
                                scalar.wait_ge(c.sem(P + "sc"), idx - 3)
                            scalar.activation(dab[idx % 4][:],
                                              deltab[ch % 2][:, dc * TCH:(dc + 1) * TCH],
                                              AF.Exp,
                                              scale=g["acol"][:, dc * NST + n: dc * NST + n + 1]
                                              ).then_inc(c.sem(P + "da"), 1)
                            if idx >= 4 and idx - 4 < (NCH - 1) * NT:
                                pn, pdc = divmod((idx - 4) % NT, NDC)
                                scalar.activation(
                                    g["hstate"][:, pdc * NST + pn: pdc * NST + pn + 1],
                                    hb[(idx - 4) % 4][:, TCH - 1: TCH], AF.Identity)
                for e0 in range(4 * NTC2):
                    evac(scalar, NCH - 1, e0)

            def bc_pair(gp, ch2, n2):
                gp.dma_start(brepb[n2 % 2][:],
                             bass.AP(scr["bc"].tensor, n2 * T + ch2 * TCH,
                                     [[0, 128], [1, TCH]])
                             ).then_inc(c.sem(P + "bcl"), 16)
                gp.dma_start(crepb[n2 % 2][:],
                             bass.AP(scr["bc"].tensor, (16 + n2) * T + ch2 * TCH,
                                     [[0, 128], [1, TCH]])
                             ).then_inc(c.sem(P + "bcl"), 16)

            @block.gpsimd
            def _(gp):
                for ch in range(NCH):
                    for n in range(NST):
                        k = ch * NST + n
                        if k == 0:
                            gp.wait_ge(c.sem("md" + "dst"), 16 * (NDC * 4 + 1))
                            bc_pair(gp, 0, 0)
                        if k + 1 < NCH * NST:
                            gp.wait_ge(c.sem(P + "sc"), k * NDC + 12)
                            bc_pair(gp, (k + 1) // NST, (k + 1) % NST)
                        if ch >= 1 and n in (2, 4, 6, 8):
                            _me_add_stores(c, gp, ch - 1, n // 2 - 1, n // 2, ob,
                                           deltab[(ch - 1) % 2], x_dst, TCH, NTC2)
                    if ch >= 1:
                        _me_add_stores(c, gp, ch - 1, 4, 4 * NTC2, ob,
                                       deltab[(ch - 1) % 2], x_dst, TCH, NTC2)
                _me_add_stores(c, gp, NCH - 1, 0, 4 * NTC2, ob,
                               deltab[(NCH - 1) % 2], x_dst, TCH, NTC2)

            @block.vector
            def _(vector):
                for ch in range(NCH):
                    vector.wait_ge(c.sem(P + "ld%d" % (ch % 2)), 16 * 2 * (ch // 2 + 1))
                    vector.tensor_mul(dub[:], deltab[ch % 2][:], ub[ch % 2][:]
                                      ).then_inc(c.sem(P + "du"), 1)
                    for n in range(NST):
                        vector.wait_ge(c.sem(P + "bcl"), 32 * (ch * NST + n + 1))
                        for dc in range(NDC):
                            idx = ch * NT + n * NDC + dc
                            if dc % 4 == 0:
                                a0 = dub[:, dc * TCH:(dc + 4) * TCH]
                                a1 = brepb[n % 2][:]
                                ao = dbub[:]
                                vector.tensor_mul(
                                    bass.AP(ao.tensor, ao.offset,
                                            [ao.ap[0], [TCH, 4], [1, TCH]]),
                                    bass.AP(a0.tensor, a0.offset,
                                            [a0.ap[0], [TCH, 4], [1, TCH]]),
                                    bass.AP(a1.tensor, a1.offset,
                                            [a1.ap[0], [0, 4], [1, TCH]]))
                            vector.wait_ge(c.sem(P + "da"), idx + 1)
                            vector.tensor_tensor_scan(
                                hb[idx % 4][:],
                                dab[idx % 4][:],
                                dbub[:, (idx % 4) * TCH:(idx % 4 + 1) * TCH],
                                g["hstate"][:, dc * NST + n: dc * NST + n + 1],
                                ALU.mult, ALU.add).then_inc(c.sem(P + "sc"), 1)
                            if n == 0:
                                vector.tensor_mul(yb[:, dc * TCH:(dc + 1) * TCH],
                                                  hb[idx % 4][:], crepb[n % 2][:])
                            else:
                                vector.tensor_mul(ytmp[:], hb[idx % 4][:], crepb[n % 2][:])
                                vector.tensor_add(yb[:, dc * TCH:(dc + 1) * TCH],
                                                  yb[:, dc * TCH:(dc + 1) * TCH], ytmp[:])
                    vector.wait_ge(c.sem(P + "ld"), 16 * 2 * (ch + 1))
                    if ch >= 1:
                        vector.wait_ge(c.sem(P + "wmm"), ch * 4 * NTC2)
                    for dc in range(NDC):
                        vector.scalar_tensor_tensor(
                            yb[:, dc * TCH:(dc + 1) * TCH], ub[ch % 2][:, dc * TCH:(dc + 1) * TCH],
                            g["dvec"][:, dc:dc + 1], yb[:, dc * TCH:(dc + 1) * TCH],
                            ALU.mult, ALU.add)
                        vector.tensor_mul(ysr[:, dc * TCH:(dc + 1) * TCH],
                                          yb[:, dc * TCH:(dc + 1) * TCH],
                                          srb[:, dc * TCH:(dc + 1) * TCH])
                        vector.tensor_copy(yh[:, dc * TCH:(dc + 1) * TCH],
                                           ysr[:, dc * TCH:(dc + 1) * TCH])
                        vector.tensor_sub(yl[:, dc * TCH:(dc + 1) * TCH],
                                          ysr[:, dc * TCH:(dc + 1) * TCH],
                                          yh[:, dc * TCH:(dc + 1) * TCH])
                    vector.nop().then_inc(c.sem(P + "fin"), 1)

            @block.tensor
            def _(tensor):
                for ch in range(NCH):
                    _me_wout_mms(c, tensor, ch, yh, yl, wohb, wolb, xsb,
                                 g, pw, TCH, NTC2)


def _me_wout_mms(c, tensor, ch, yh, yl, wohb, wolb, xsb, g, pw, TCH, NTC2):
    P = "me"
    tensor.wait_ge(c.sem(P + "fin"), ch + 1)
    tensor.wait_ge(c.sem(P + "ld"), 16 * 2 * (ch + 1))
    for dg in range(4):
        j = ch * 4 + dg
        tensor.wait_ge(c.sem(P + "wl"), 32 * (j + 1))
        for i in range(NTC2):
            e = ch * 4 * NTC2 + dg * NTC2 + i
            if e >= 2:
                tensor.wait_ge(c.sem(P + "ev"), e - 1)
            first = True
            for dc in range(NDC):
                for (lt, rt) in ((yh, wohb), (yh, wolb), (yl, wohb)):
                    tensor.matmul(
                        pw[i % 2][:],
                        lt[:, dc * TCH + i * 128: dc * TCH + (i + 1) * 128],
                        rt[:, dc * 512:(dc + 1) * 512],
                        start=first, stop=False)
                    first = False
            for jj in range(4):
                oc = dg * 4 + jj
                ins = tensor.matmul(
                    pw[i % 2][:, jj * 128:(jj + 1) * 128],
                    xsb[:, oc * TCH + i * 128: oc * TCH + (i + 1) * 128],
                    g["eye"][:],
                    start=False, stop=(jj == 3))
            ins.then_inc(c.sem(P + "wmm"), 1)


def _me_add_stores(c, gp, ch, e0lo, e0hi, ob, x1b, x_dst, TCH, NTC2):
    P = "me"
    if e0lo == 0:
        gp.wait_ge(c.sem(P + "sel"), 16 * (ch + 1))
    for e0 in range(e0lo, e0hi):
        dg, i = divmod(e0, NTC2)
        e = ch * 4 * NTC2 + e0
        gp.wait_ge(c.sem(P + "ev"), e + 1)
        gp.tensor_add(ob[e % 4][:], ob[e % 4][:],
                      x1b[:, i * D + dg * 512: i * D + dg * 512 + 512])
        gp.dma_start(rows_src(x_dst, ch * TCH + i * 128, 128, 512, D, dg * 512),
                     ob[e % 4][:]).then_inc(c.sem(P + "ost"), 16)


def build_program(dbg=()):
    nc = bass.Bass("TRN2", target_bir_lowering=False, detect_race_conditions=False)

    io = {}
    io["x_in"] = nc.dram_tensor("x_in", [T, D], F32, kind="ExternalInput").ap()
    io["w1t"] = nc.dram_tensor("w1t", [D, DFF], F32, kind="ExternalInput").ap()
    io["w2t"] = nc.dram_tensor("w2t", [DFF, D], F32, kind="ExternalInput").ap()
    io["w_in"] = nc.dram_tensor("w_in", [D, 2 * DIN], F32, kind="ExternalInput").ap()
    io["convw"] = nc.dram_tensor("convw", [KCV * DIN, DIN], F32, kind="ExternalInput").ap()
    io["wx"] = nc.dram_tensor("wx", [DIN, DTR + 2 * NST], F32, kind="ExternalInput").ap()
    io["wdt"] = nc.dram_tensor("wdt", [DTR, DIN], F32, kind="ExternalInput").ap()
    io["bdt"] = nc.dram_tensor("bdt", [DIN, 1], F32, kind="ExternalInput").ap()
    io["convb"] = nc.dram_tensor("convb", [DIN, 1], F32, kind="ExternalInput").ap()
    io["alog"] = nc.dram_tensor("alog", [DIN, NST], F32, kind="ExternalInput").ap()
    io["dvec"] = nc.dram_tensor("dvec", [DIN, 1], F32, kind="ExternalInput").ap()
    io["wout"] = nc.dram_tensor("wout", [D, D], F32, kind="ExternalInput").ap()
    io["eye"] = nc.dram_tensor("eye", [128, 128], F32, kind="ExternalInput").ap()
    io["ones1"] = nc.dram_tensor("ones1", [1, 128], F32, kind="ExternalInput").ap()
    io["y_out"] = nc.dram_tensor("y_out", [T, D], F32, kind="ExternalOutput").ap()

    def dscr(name, shape, dtype):
        kind = "ExternalOutput" if name in dbg else "Internal"
        return nc.dram_tensor(name, shape, dtype, kind=kind).ap()

    scr = {}
    scr["x1"] = dscr("scr_x1", [T, D], F32)
    scr["x2"] = dscr("scr_x2", [T, D], F32)
    scr["iq"] = dscr("scr_iq", [T, D], BF16)
    scr["h"] = dscr("scr_h", [T, DFF], F32)
    scr["ih"] = dscr("scr_ih", [T, DFF], BF16)
    scr["iw1"] = dscr("scr_iw1", [D, DFF], BF16)
    scr["iw2"] = dscr("scr_iw2", [DFF, D], BF16)
    scr["xch"] = dscr("scr_xch", [DIN, T], BF16)
    scr["xcl"] = dscr("scr_xcl", [DIN, T], BF16)
    scr["sr"] = dscr("scr_sr", [DIN, T], F32)
    scr["xs"] = dscr("scr_xs", [DIN, T], F32)
    scr["u"] = dscr("scr_u", [DIN, T], F32)
    scr["uh"] = dscr("scr_uh", [DIN, T], BF16)
    scr["ul"] = dscr("scr_ul", [DIN, T], BF16)
    scr["delta"] = dscr("scr_delta", [DIN, T], F32)
    scr["bc"] = dscr("scr_bc", [32, T], F32)
    scr["wouth"] = dscr("scr_wouth", [D, D], BF16)
    scr["woutl"] = dscr("scr_woutl", [D, D], BF16)

    es = contextlib.ExitStack()
    with es:
        c = Ctx(nc, es)
        g = {}
        g["eye"] = es.enter_context(nc.sbuf_tensor("g_eye", [128, 128], F32))
        g["ones1"] = es.enter_context(nc.sbuf_tensor("g_ones1", [1, 128], F32))
        g["c1536"] = es.enter_context(nc.sbuf_tensor("g_c1536", [128, 1], F32))
        g["ceps"] = es.enter_context(nc.sbuf_tensor("g_ceps", [128, 1], F32))
        g["sw1"] = es.enter_context(nc.sbuf_tensor("g_sw1", [128, 1], F32))
        g["rw1"] = es.enter_context(nc.sbuf_tensor("g_rw1", [128, 1], F32))
        g["sw2"] = es.enter_context(nc.sbuf_tensor("g_sw2", [128, 1], F32))
        g["rw2"] = es.enter_context(nc.sbuf_tensor("g_rw2", [128, 1], F32))
        g["dq1"] = es.enter_context(nc.sbuf_tensor("g_dq1", [128, NTC], F32))
        g["dq2h"] = es.enter_context(nc.sbuf_tensor("g_dq2h", [128, NTC], F32))
        g["hmax"] = es.enter_context(nc.sbuf_tensor("g_hmax", [128, NTC * NFG], F32))
        g["acol"] = es.enter_context(nc.sbuf_tensor("g_acol", [128, NDC * NST], F32))
        g["bdt"] = es.enter_context(nc.sbuf_tensor("g_bdt", [128, NDC], F32))
        g["convb"] = es.enter_context(nc.sbuf_tensor("g_convb", [128, NDC], F32))
        g["dvec"] = es.enter_context(nc.sbuf_tensor("g_dvec", [128, NDC], F32))
        g["hstate"] = es.enter_context(nc.sbuf_tensor("g_hstate", [128, NDC * NST], F32))

        _emit_p0(c, io, g, scr)
        _emit_ffn(c, 1, dict(io, x_src=io["x_in"],
                             x_dst=(io["y_out"] if BUILD_PHASES == 1 else scr["x1"])),
                  g, scr)
        if BUILD_PHASES >= 2:
            _emit_mamba(c, dict(io, x_src=scr["x1"],
                                x_dst=(io["y_out"] if BUILD_PHASES == 2 else scr["x2"])),
                        g, scr)
        if BUILD_PHASES >= 3:
            _emit_ffn(c, 2, dict(io, x_src=scr["x2"], x_dst=io["y_out"]), g, scr)

        with nc.Block("fin") as block:
            @block.sync
            def _(sync):
                for s, n in list(c.cnt.items()):
                    if s.endswith("ost") and n:
                        sync.wait_ge(c.sems[s], n)
    return nc


def kernel(L0_attn_x, L0_attn_W_in, L0_attn_conv_w, L0_attn_conv_b,
           L0_attn_W_x, L0_attn_W_dt, L0_attn_b_dt, L0_attn_A_log,
           L0_attn_D, L0_attn_W_out, L0_ffn_W1, L0_ffn_W2):
    x = np.ascontiguousarray(L0_attn_x, dtype=np.float32)
    shared = dict(
        w1t=np.ascontiguousarray(np.asarray(L0_ffn_W1, np.float32).T),
        w2t=np.ascontiguousarray(np.asarray(L0_ffn_W2, np.float32).T),
        w_in=np.ascontiguousarray(L0_attn_W_in, np.float32),
        convw=np.ascontiguousarray(
            np.asarray(L0_attn_conv_w, np.float32).transpose(2, 1, 0).reshape(KCV * DIN, DIN)),
        wx=np.ascontiguousarray(L0_attn_W_x, np.float32),
        wdt=np.ascontiguousarray(L0_attn_W_dt, np.float32),
        bdt=np.ascontiguousarray(L0_attn_b_dt, np.float32).reshape(DIN, 1),
        convb=np.ascontiguousarray(L0_attn_conv_b, np.float32).reshape(DIN, 1),
        alog=np.ascontiguousarray(L0_attn_A_log, np.float32),
        dvec=np.ascontiguousarray(L0_attn_D, np.float32).reshape(DIN, 1),
        wout=np.ascontiguousarray(L0_attn_W_out, np.float32),
        eye=np.eye(128, dtype=np.float32),
        ones1=np.ones((1, 128), np.float32),
    )
    nc = build_program()
    in_maps = [dict(shared, x_in=np.ascontiguousarray(x[b])) for b in range(8)]
    res = run_bass_kernel_spmd(nc, in_maps, core_ids=list(range(8)))
    return np.stack([r["y_out"] for r in res.results], axis=0)
